# version 10
# baseline (speedup 1.0000x reference)
import math
from collections import defaultdict
from contextlib import contextmanager
from types import SimpleNamespace

import numpy as np
import ml_dtypes

import concourse.bass as bass
import concourse.mybir as mybir
from concourse.bass_utils import run_bass_kernel_spmd

BF = ml_dtypes.bfloat16
F32 = mybir.dt.float32
BF16 = mybir.dt.bfloat16
AF = mybir.ActivationFunctionType
ALU = mybir.AluOpType
AX = mybir.AxisListType

T, D = 4096, 1024
H4 = 4096
HALF = 512
SCALE = 1.0 / 32.0
NEG = -1.0e30
NCORES = 8

ENG_ATTR = {"sp": "sync", "pe": "tensor", "act": "scalar", "dve": "vector", "pool": "gpsimd"}
ENGS = list(ENG_ATTR)


class Plan:
    def __init__(self, nc):
        self.nc = nc
        self.items = {k: [] for k in ENGS}
        self.cnt = defaultdict(int)
        self.lastwait = {k: defaultdict(int) for k in ENGS}
        self.sem_names = []

    def _sem(self, name):
        if name not in self.cnt:
            self.sem_names.append(name)
            self.cnt[name] = 0

    def op(self, eng, emit, deps=(), inc=None, by=1):
        waits = []
        for d in deps:
            if d is None:
                continue
            s, v = d
            self._sem(s)
            if v > self.lastwait[eng][s]:
                waits.append((s, v))
                self.lastwait[eng][s] = v
        tok = None
        if inc is not None:
            self._sem(inc)
            self.cnt[inc] += by
            tok = (inc, self.cnt[inc])
        self.items[eng].append((waits, emit, inc, by))
        return tok

    def dma(self, q, out_ap, in_ap, sem, deps=(), **kw):
        return self.op(
            q, (lambda e, o=out_ap, i=in_ap, kw=dict(kw): e.dma_start(o, i, **kw)),
            deps, inc=sem, by=16)

    def all_tokens(self):
        return [(s, self.cnt[s]) for s in self.sem_names if self.cnt[s] > 0]

    def barrier(self):
        toks = self.all_tokens()
        for e in ENGS:
            self.op(e, None, deps=toks)

    def emit(self, blk, sems):
        for name, attr in ENG_ATTR.items():
            items = self.items[name]

            def make(items=items):
                def f(eng):
                    for waits, emitfn, inc, by in items:
                        ins = None
                        for (s, v) in waits:
                            ins = eng.wait_ge(sems[s], v)
                        if emitfn is not None:
                            ins = emitfn(eng)
                        if inc is not None:
                            ins.then_inc(sems[inc], by)
                return f

            getattr(blk, attr)(make())


@contextmanager
def scope(nc):
    sb, pb = nc.sbuf_base, nc.psum_base
    yield
    nc.sbuf_base = sb
    nc.psum_base = pb


class TD:
    def __init__(self, C, name):
        self.C, self.name, self.i = C, name, 0
        C.pl._sem(name)

    def start(self):
        if self.i >= 2:
            tok = (self.name, 5 * (self.i - 1))
            for e in ENGS:
                self.C.pl.op(e, None, deps=[tok])

    def end(self):
        for e in ENGS:
            self.C.pl.op(e, lambda en: en.nop(), inc=self.name)
        self.i += 1


def MM(C, out, lhsT, rhs, start, stop, deps=(), inc=None):
    return C.pl.op(
        "pe",
        (lambda e, o=out, l=lhsT, r=rhs, st=start, sp=stop:
         e.matmul(o, l, r, start=st, stop=sp)),
        deps, inc=inc)


def TR(C, out, in_, deps=(), inc=None):
    return C.pl.op(
        "pe", (lambda e, o=out, i=in_: e.transpose(o, i, C.t_ident[:, :])), deps, inc=inc)


def OP(C, eng, emit, deps=(), inc="auto"):
    if inc == "auto":
        inc = {"pe": "Epe", "act": "Eact", "dve": "Edve", "pool": "Epool", "sp": None}[eng]
    return C.pl.op(eng, emit, deps, inc=inc)


def bcast_row(C, row_sb, out_sb, ps, deps):
    MM(C, ps[:, 0:512], C.t_posrow[0:1, :], row_sb[0:1, 0:512], True, True, deps=deps)
    t1 = MM(C, ps[:, 512:1024], C.t_posrow[0:1, :], row_sb[0:1, 512:1024], True, True,
            deps=deps, inc="Epe")
    return OP(C, "act", lambda e, o=out_sb, i=ps: e.copy(o[:, :], i[:, :]), deps=[t1])


def build_nc(cfg):
    nc = bass.Bass("TRN2", target_bir_lowering=False, debug=False)
    pl = Plan(nc)
    C = SimpleNamespace(nc=nc, pl=pl, cfg=cfg)

    def inp(name, shape, dt=F32):
        t = nc.dram_tensor(name, list(shape), dt, kind="ExternalInput")
        setattr(C, "i_" + name, t)
        return t

    inp("x", (2, T, D))
    for n, sh in [("wq", (D, D)), ("wk", (D, D)), ("wv", (D, D)), ("wgate", (D, D)),
                  ("wp", (D, D)), ("wclock", (D, 1)), ("wo", (D, D)),
                  ("w1m", (D, H4)), ("w2m", (H4, D)),
                  ("fw1t", (D, H4)), ("fw2t", (D, H4)), ("fh1r", (32, 128, 32, 128)),
                  ("fh2r", (32, 128, 32, 128)), ("fw3t", (H4, D)),
                  ("gbrow", (1, D)), ("pbrow", (1, D)), ("obrow", (1, D)),
                  ("b2mrow", (1, D)), ("s3row", (1, D)), ("cbrow", (1, 1)),
                  ("g1c", (128, 8)), ("b1c", (128, 8)), ("g2c", (128, 8)), ("b2c", (128, 8)),
                  ("b1m_pc", (128, 32)), ("s1_pc", (128, 32)), ("s2_pc", (128, 32)),
                  ("h1s_pc", (128, 32)), ("h2s_pc", (128, 32)),
                  ("ctri_n", (128, 128)), ("ctri_p", (128, 128)),
                  ("cnegrow", (1, 128)), ("cposrow", (1, 128)), ("conescol", (128, 1)),
                  ("cmasktri", (128, 128)), ("ccos", (T, HALF)), ("csin", (T, HALF)),
                  ("cbias", (128, 3))]:
        inp(n, sh)
    inp("cident", (128, 128), BF16)

    C.out_t = nc.dram_tensor("out", [2, T, D], F32, kind="ExternalOutput")

    def scr(name, shape, dt):
        kind = "ExternalOutput" if name in getattr(cfg, "dev_outputs", ()) else "Internal"
        t = nc.dram_tensor(name, list(shape), dt, kind=kind)
        setattr(C, name, t)
        return t

    for n, sh, dt in [("d_h1R", (32, 128, H4), BF16), ("d_h2R", (32, 128, H4), BF16),
                      ("a_xn", (2, T, D), F32), ("a_xnT", (2, 8, 128, T), BF16),
                      ("a_xn2T", (2, 8, 128, T), BF16), ("a_A", (2, T, D), F32),
                      ("a_ph", (2, T, D), F32), ("a_clock", (2, 128, 32), F32),
                      ("a_pmaxrow", (2, D), F32), ("a_qfT", (2, 8, 128, T), BF16),
                      ("a_kfT", (2, 8, 128, T), BF16), ("a_v", (2, T, D), BF16),
                      ("a_x2", (2, T, D), F32), ("a_hT", (2, 8, 128, T), BF16),
                      ("a_x3", (2, T, D), F32), ("a_h", (2, 4, 128, 32, 1024), BF16),
                      ("a_g1", (2, 4, 128, 32, 1024), BF16),
                      ("a_g2", (2, 4, 128, 32, 1024), BF16)]:
        scr(n, sh, dt)

    C.v_x_q = C.i_x.ap().rearrange("s (q p) d -> s q p d", p=128)
    C.v_x_t = C.i_x.ap().rearrange("s (t u p) d -> s t p u d", p=128, u=4)
    C.v_out_t = C.out_t.ap().rearrange("s (t u p) d -> s t p u d", p=128, u=4)
    C.v_xn_q = C.a_xn.ap().rearrange("s (q p) d -> s q p d", p=128)
    C.v_xn_t = C.a_xn.ap().rearrange("s (t u p) d -> s t p u d", p=128, u=4)
    C.v_A_t = C.a_A.ap().rearrange("s (t u p) d -> s t p u d", p=128, u=4)
    C.v_A_c = C.a_A.ap().rearrange("s (c p) d -> s c p d", p=128)
    C.v_ph_t = C.a_ph.ap().rearrange("s (t u p) d -> s t p u d", p=128, u=4)
    C.v_ph_c = C.a_ph.ap().rearrange("s (c p) d -> s c p d", p=128)
    C.v_xnT = C.a_xnT.ap().rearrange("s c p t -> s p c t")
    C.v_xn2T = C.a_xn2T.ap().rearrange("s c p t -> s p c t")
    C.v_qfT = C.a_qfT.ap().rearrange("s c p t -> s p c t")
    C.v_kfT = C.a_kfT.ap().rearrange("s c p t -> s p c t")
    C.v_v_t = C.a_v.ap().rearrange("s (t u p) d -> s t p u d", p=128, u=4)
    C.v_v_g = C.a_v.ap().rearrange("s (g u p) d -> s g p u d", p=128, u=4)
    C.v_x2_q = C.a_x2.ap().rearrange("s (q p) d -> s q p d", p=128)
    C.v_hT = C.a_hT.ap().rearrange("s c p t -> s p c t")
    C.v_x3_q = C.a_x3.ap().rearrange("s (q p) d -> s q p d", p=128)
    C.v_h = C.a_h.ap()
    C.v_g1 = C.a_g1.ap()
    C.v_g2 = C.a_g2.ap()
    C.v_cos = C.i_ccos.ap().rearrange("(c p) h -> c p h", p=128)
    C.v_sin = C.i_csin.ap().rearrange("(c p) h -> c p h", p=128)

    def sba(name, shape, dt):
        return nc.alloc_sbuf_tensor(name, list(shape), dt)

    C.t_trin = sba("c_trin", (128, 128), F32)
    C.t_trip = sba("c_trip", (128, 128), F32)
    C.t_negrow = sba("c_negrow", (1, 128), F32)
    C.t_posrow = sba("c_posrow", (1, 128), F32)
    C.t_onescol = sba("c_onescol", (128, 1), F32)
    C.t_ident = sba("c_ident", (128, 128), BF16)
    C.t_mask = sba("c_mask", (128, 128), F32)
    C.t_cbias = sba("c_cbias", (128, 3), F32)
    nc.const_aps.register(F32, 1e-5, C.t_cbias[:, 0:1])
    nc.const_aps.register(F32, 1.0, C.t_cbias[:, 1:2])
    nc.const_aps.register(F32, 1e-8, C.t_cbias[:, 2:3])

    tk = None
    for dst, srct in [(C.t_trin, C.i_ctri_n), (C.t_trip, C.i_ctri_p),
                      (C.t_negrow, C.i_cnegrow), (C.t_posrow, C.i_cposrow),
                      (C.t_onescol, C.i_conescol), (C.t_ident, C.i_cident),
                      (C.t_mask, C.i_cmasktri), (C.t_cbias, C.i_cbias)]:
        tk = pl.dma("sp", dst[:, :], srct.ap(), "Dc")
    C.tok_consts = tk

    if "p0" in cfg.passes:
        for srct, dstt in [(C.i_fh1r, C.d_h1R), (C.i_fh2r, C.d_h2R)]:
            for i in range(4):
                sl = slice(i * 8, (i + 1) * 8)
                pl.dma("pool", dstt.ap()[sl],
                       srct.ap()[sl].rearrange("o p c f -> o p (c f)"), "Dp0")

    if "p1" in cfg.passes:
        with scope(nc):
            plan_p12(C)
        pl.barrier()
    if "p3" in cfg.passes:
        with scope(nc):
            plan_p3(C)
        pl.barrier()
    if "p4" in cfg.passes:
        with scope(nc):
            plan_p4(C)
        pl.barrier()
    if "p5" in cfg.passes:
        with scope(nc):
            plan_p5(C)
        pl.barrier()
    if "p6" in cfg.passes:
        with scope(nc):
            plan_p67(C, 1)
        pl.barrier()
        with scope(nc):
            plan_p67(C, 2)
        pl.barrier()
    if "p8" in cfg.passes:
        with scope(nc):
            plan_p8(C)
    pl.barrier()

    sems = {}
    cms = []
    for s in pl.sem_names:
        cm = nc.semaphore(s)
        sems[s] = cm.__enter__()
        cms.append(cm)
    for s in pl.sem_names:
        nc.gpsimd.sem_clear(sems[s])
    nc.all_engine_barrier()
    with nc.Block() as blk:
        pl.emit(blk, sems)
    for s in pl.sem_names:
        nc.gpsimd.sem_clear(sems[s])
    nc.all_engine_barrier()
    for cm in reversed(cms):
        cm.__exit__(None, None, None)
    return nc, C


def plan_p12(C):
    nc, pl = C.nc, C.pl

    def sb(shape, dt):
        C._uid = getattr(C, "_uid", 0) + 1
        return nc.alloc_sbuf_tensor(f"sb{C._uid}", list(shape), dt)

    def ps(shape, dt):
        C._uid = getattr(C, "_uid", 0) + 1
        return nc.alloc_psum_tensor(f"ps{C._uid}", list(shape), dt)

    carry = sb((1, D), F32)
    rowt = sb((1, D), F32)
    C.pmax_b = [sb((128, D), F32), sb((128, D), F32)]

    pj0 = ps((128, D), F32)
    pj1 = ps((128, D), F32)
    sc = ps((128, D), F32)
    tp = [ps((128, D), BF16), ps((128, D), BF16)]

    sc1 = scope(nc); sc1.__enter__()
    w_g = sb((128, 8, D), BF16)
    w_p = sb((128, 8, D), BF16)
    w_c = sb((128, 8, 1), BF16)
    r_row = sb((1, D), F32); r_cb = sb((1, 1), F32)
    t_g1c = sb((128, 8), F32); t_b1c = sb((128, 8), F32)
    gb_b = sb((128, D), F32); pb_b = sb((128, D), F32)
    cb_c = sb((128, 1), F32)

    pl.dma("pool", w_g[:, :, :], C.i_wgate.ap().rearrange("(c p) o -> p c o", p=128), "Dw")
    tok_w = pl.dma("pool", w_p[:, :, :], C.i_wp.ap().rearrange("(c p) o -> p c o", p=128), "Dw")
    tok_wc = pl.dma("pool", w_c[:, :, :], C.i_wclock.ap().rearrange("(c p) o -> p c o", p=128), "Dw",
                    allow_slow_non_contiguous=True)
    tr0 = pl.dma("sp", r_row[0:1, :], C.i_gbrow.ap(), "Dc")
    tr2 = pl.dma("sp", r_cb[0:1, :], C.i_cbrow.ap(), "Dc")
    tr3 = pl.dma("sp", t_g1c[:, :], C.i_g1c.ap(), "Dc")
    tr4 = pl.dma("sp", t_b1c[:, :], C.i_b1c.ap(), "Dc")
    tk_gb = bcast_row(C, r_row, gb_b, pj0, deps=[tr0, C.tok_consts])
    tr1 = pl.dma("sp", r_row[0:1, :], C.i_pbrow.ap(), "Dc", deps=[tk_gb])
    tk_pb = bcast_row(C, r_row, pb_b, pj1, deps=[tr1])
    t_cb1 = MM(C, sc[:, 0:1], C.t_posrow[0:1, :], r_cb[0:1, 0:1], True, True,
               deps=[tr2, C.tok_consts], inc="Epe")
    tk_cb = OP(C, "act", lambda e: e.copy(cb_c[:, :], sc[:, 0:1]), deps=[t_cb1])

    xin = [sb((128, 4, D), F32), sb((128, 4, D), F32)]
    xn_sb = sb((128, 4, D), F32)
    bf1 = sb((128, D), BF16)
    bf2 = sb((128, D), BF16)
    xnT_st = sb((128, 8, 512), BF16)
    xn2T_st = sb((128, 8, 512), BF16)
    g_sb = sb((128, 4, D), F32)
    scr_ = sb((128, 4, D), F32)
    p_sb = sb((128, 4, D), F32)
    A_sb = sb((128, 4, D), F32)
    yb = sb((128, 16), F32)
    clk_all = sb((128, 32), F32)
    M_sb = sb((128, D), F32)
    pmaxT = sb((128, 8), F32)
    prow = sb((1, D), F32)
    stt = sb((128, 32, 1), F32)

    td = TD(C, "T1")
    xn_store = None; xnT_store = None; xn2T_store = None
    A_store = None; p_store = None; clk_store = None
    pj0_free = tk_gb; pj1_free = tk_pb; sc_free = tk_cb
    carry_tok = None
    M_free = None
    bf1_free = None; bf2_free = None
    tp_last = [None, None]
    scr_free = None
    g_free = None
    sp_free = None
    xn_free = None
    tA_prev = None
    clk_tok = None
    pmaxT_free = None

    for s in range(2):
        t_m0 = OP(C, "pool", lambda e: e.memset(M_sb[:, :], -3.0e38), deps=[M_free])
        carry_tok = OP(C, "dve", lambda e: e.memset(carry[0:1, :], 0.0), deps=[carry_tok])
        M_tok = t_m0

        for tt in range(8):
            td.start()
            b = tt % 2
            gi = s * 8 + tt
            pb2 = 16 * (gi % 2)
            s1 = stt[:, pb2 + 0:pb2 + 4, :]; ssq = stt[:, pb2 + 4:pb2 + 8, :]
            mcol = stt[:, pb2 + 8:pb2 + 12, :]; rstd = stt[:, pb2 + 12:pb2 + 16, :]
            yo = 8 * (gi % 2)
            ybuf = yb[:, yo:yo + 4]; ysp = yb[:, yo + 4:yo + 8]
            tload = pl.dma("sp", xin[b][:, :, :], C.v_x_t[s, tt], f"Dx{b}",
                           deps=[tk_gb, tk_pb, tk_cb] if gi == 0 else ())

            def ln_stats(src_ap, dep):
                t_s1 = OP(C, "dve", lambda e: e.reduce_sum(s1[:, :, :], src_ap, axis=AX.X),
                          deps=[dep])
                t_sq = OP(C, "act", lambda e: e.activation(scr_[:, :, :], src_ap, AF.Square),
                          deps=[dep, scr_free])
                t_ssq = OP(C, "dve", lambda e: e.reduce_sum(ssq[:, :, :], scr_[:, :, :], axis=AX.X),
                           deps=[t_sq])
                t_m = OP(C, "dve", lambda e: e.tensor_scalar_mul(mcol[:, :, :], s1[:, :, :], 1.0 / D),
                         deps=[t_s1])
                t_v = OP(C, "dve", lambda e: e.tensor_scalar(
                    ssq[:, :, :], ssq[:, :, :], 1.0 / D, None, ALU.mult), deps=[t_ssq])
                t_mm = OP(C, "dve", lambda e: e.tensor_mul(s1[:, :, :], mcol[:, :, :], mcol[:, :, :]),
                          deps=[t_m])
                t_v = OP(C, "dve", lambda e: e.tensor_sub(ssq[:, :, :], ssq[:, :, :], s1[:, :, :]),
                         deps=[t_v, t_mm])
                t_l = OP(C, "act", lambda e: e.activation(s1[:, :, :], ssq[:, :, :], AF.Ln, bias=1e-5),
                         deps=[t_v, C.tok_consts])
                t_r = OP(C, "act", lambda e: e.activation(rstd[:, :, :], s1[:, :, :], AF.Exp, scale=-0.5),
                         deps=[t_l])
                return t_r, t_ssq

            t_r1, scr_free = ln_stats(xin[b][:, :, :], tload)
            t_xn = []
            for u in range(4):
                t_xn.append(OP(
                    C, "dve",
                    lambda e, b=b, u=u: e.tensor_scalar(
                        xn_sb[:, u, :], xin[b][:, u, :], mcol[:, u, :], rstd[:, u, :],
                        ALU.subtract, ALU.mult),
                    deps=[t_r1, xn_store, xn_free]))
            xn_store = pl.dma("sp", C.v_xn_t[s, tt], xn_sb[:, :, :], "Sxn", deps=[t_xn[-1]])

            t_a = OP(C, "dve", lambda e: e.tensor_mul(
                s1[:, :, :], rstd[:, :, :], rstd[:, :, :]), deps=[t_r1])
            t_b = OP(C, "dve", lambda e: e.tensor_mul(
                ssq[:, :, :], ssq[:, :, :], s1[:, :, :]), deps=[t_a, scr_free])
            t_c = OP(C, "act", lambda e: e.activation(
                s1[:, :, :], ssq[:, :, :], AF.Ln, bias=1e-5), deps=[t_b, C.tok_consts])
            t_r2 = OP(C, "act", lambda e: e.activation(
                mcol[:, :, :], s1[:, :, :], AF.Exp, scale=-0.5),
                deps=[t_c, t_xn[-1]])
            t_xnn = []
            for u in range(4):
                t_xnn.append(OP(
                    C, "dve",
                    lambda e, b=b, u=u: e.tensor_scalar_mul(
                        xin[b][:, u, :], xn_sb[:, u, :], mcol[:, u, :]),
                    deps=[t_r2]))

            tpA = tp[gi % 2]
            x2st = xn2T_st
            ev1_toks = []; ev2_toks = []
            tp_free = tp_last[gi % 2]
            for u in range(4):
                t_c1 = OP(C, "act", lambda e, u=u: e.copy(bf1[:, :], xn_sb[:, u, :]),
                          deps=[t_xn[u], bf1_free])
                tr_ = None
                for ic in range(8):
                    tr_ = TR(C, tpA[:, ic * 128:(ic + 1) * 128],
                             bf1[:, ic * 128:(ic + 1) * 128],
                             deps=[t_c1, tp_free, C.tok_consts] if ic == 0 else (),
                             inc="Epe" if ic == 7 else None)
                bf1_free = tr_
                ev1 = OP(C, "dve",
                         lambda e, u=u, tpA=tpA: e.tensor_copy(
                             xnT_st[:, :, u * 128:(u + 1) * 128],
                             tpA[:, :].rearrange("p (c f) -> p c f", f=128)),
                         deps=[tr_, xnT_store])
                ev1_toks.append(ev1)

                t_c2 = OP(C, "act", lambda e, b=b, u=u: e.copy(bf2[:, :], xin[b][:, u, :]),
                          deps=[t_xnn[u], bf2_free])
                tr2_ = None
                for ic in range(8):
                    tr2_ = TR(C, tpA[:, ic * 128:(ic + 1) * 128],
                              bf2[:, ic * 128:(ic + 1) * 128],
                              deps=[t_c2, ev1] if ic == 0 else (),
                              inc="Epe" if ic == 7 else None)
                bf2_free = tr2_
                last = None
                for ic in range(8):
                    last = OP(C, "dve",
                              lambda e, u=u, ic=ic, x2st=x2st, tpA=tpA: e.tensor_scalar(
                                  x2st[:, ic, u * 128:(u + 1) * 128],
                                  tpA[:, ic * 128:(ic + 1) * 128],
                                  t_g1c[:, ic:ic + 1], t_b1c[:, ic:ic + 1],
                                  ALU.mult, ALU.add),
                              deps=[tr2_, xn2T_store, tr3, tr4])
                tp_free = last
                ev2_toks.append(last)

            tp_last[gi % 2] = tp_free
            xnT_store = pl.dma("sp", C.v_xnT[s, :, :, tt * 512:(tt + 1) * 512],
                               xnT_st[:, :, :], "SxnT", deps=ev1_toks)
            xn2T_store = pl.dma(
                "sp", C.v_xn2T[s, :, :, tt * 512:(tt + 1) * 512],
                x2st[:, :, :], "Sx2T", deps=ev2_toks)

            gev = []; pev = []; yev = []
            for u in range(4):
                tg = None
                for half in range(2):
                    for ic in range(8):
                        tg = MM(C, pj0[:, half * 512:(half + 1) * 512],
                                x2st[:, ic, u * 128:(u + 1) * 128],
                                w_g[:, ic, half * 512:(half + 1) * 512],
                                start=(ic == 0), stop=(ic == 7),
                                deps=[ev2_toks[u], tok_w, pj0_free] if (ic == 0 and half == 0) else (),
                                inc="Epe" if (ic == 7 and half == 1) else None)
                tge = OP(C, "dve",
                         lambda e, u=u: e.scalar_tensor_tensor(
                             g_sb[:, u, :], gb_b[:, :], 1.0, pj0[:, :], ALU.bypass, ALU.add),
                         deps=[tg, tk_gb, g_free])
                pj0_free = tge
                gev.append(tge)

                tp_ = None
                for half in range(2):
                    for ic in range(8):
                        tp_ = MM(C, pj1[:, half * 512:(half + 1) * 512],
                                 x2st[:, ic, u * 128:(u + 1) * 128],
                                 w_p[:, ic, half * 512:(half + 1) * 512],
                                 start=(ic == 0), stop=(ic == 7),
                                 deps=[ev2_toks[u], tok_w, pj1_free] if (ic == 0 and half == 0) else (),
                                 inc="Epe" if (ic == 7 and half == 1) else None)
                tpe = OP(C, "dve",
                         lambda e, u=u: e.scalar_tensor_tensor(
                             p_sb[:, u, :], pb_b[:, :], 1.0, pj1[:, :], ALU.bypass, ALU.add),
                         deps=[tp_, tk_pb, p_store, M_tok])
                pj1_free = tpe
                pev.append(tpe)
                M_tok = OP(C, "dve",
                           lambda e, u=u: e.tensor_max(M_sb[:, :], p_sb[:, u, :], M_sb[:, :]),
                           deps=[tpe, M_tok])

                tc = None
                for ic in range(8):
                    tc = MM(C, pj0[:, 0:1], x2st[:, ic, u * 128:(u + 1) * 128],
                            w_c[:, ic, :], start=(ic == 0), stop=(ic == 7),
                            deps=[tge, tok_wc] if ic == 0 else (),
                            inc="Epe" if ic == 7 else None)
                ty = OP(C, "dve", lambda e, u=u, ybuf=ybuf: e.tensor_add(
                    ybuf[:, u:u + 1], pj0[:, 0:1], cb_c[:, :]), deps=[tc, tk_cb, clk_tok])
                pj0_free = ty
                yev.append(ty)

            ta0 = OP(C, "act", lambda e, ysp=ysp: e.activation(ysp[:, :], ybuf[:, :], AF.Abs),
                     deps=yev + [clk_tok])
            ta = OP(C, "act", lambda e, ysp=ysp: e.activation(ysp[:, :], ysp[:, :], AF.Exp, scale=-1.0),
                    deps=[ta0])
            ta = OP(C, "act", lambda e, ysp=ysp: e.activation(ysp[:, :], ysp[:, :], AF.Ln, bias=1.0),
                    deps=[ta, C.tok_consts])
            trl = OP(C, "dve", lambda e, ybuf=ybuf: e.tensor_scalar_max(ybuf[:, :], ybuf[:, :], 0.0),
                     deps=yev + [ta0])
            clk_tok = OP(C, "pool", lambda e, tt=tt, ybuf=ybuf, ysp=ysp: e.tensor_add(
                clk_all[:, tt * 4:(tt + 1) * 4], ybuf[:, :], ysp[:, :]),
                deps=[ta, trl, clk_store])

            ta0 = OP(C, "act", lambda e: e.activation(scr_[:, :, :], g_sb[:, :, :], AF.Abs),
                     deps=gev + [scr_free])
            ta = OP(C, "act", lambda e: e.activation(scr_[:, :, :], scr_[:, :, :], AF.Exp, scale=-1.0),
                    deps=[ta0])
            ta = OP(C, "act", lambda e: e.activation(scr_[:, :, :], scr_[:, :, :], AF.Ln, bias=1.0),
                    deps=[ta, C.tok_consts])
            trl = None
            for u in range(4):
                trl = OP(C, "pool", lambda e, u=u: e.tensor_scalar_max(
                    A_sb[:, u, :], g_sb[:, u, :], 0.0),
                    deps=(gev + [ta0, A_store, tA_prev]) if u == 0 else ())
            tsp = OP(C, "pool", lambda e: e.tensor_add(g_sb[:, :, :], A_sb[:, :, :], scr_[:, :, :]),
                     deps=[ta, trl, sp_free])
            scr_free = tsp

            clip_toks = []
            for u in range(4):
                tmm = None
                for half in range(2):
                    tmm = MM(C, sc[:, half * 512:(half + 1) * 512], C.t_trin[:, :],
                             g_sb[:, u, half * 512:(half + 1) * 512], True, False,
                             deps=[tsp, sc_free, carry_tok, C.tok_consts] if half == 0 else ())
                for half in range(2):
                    tmm = MM(C, sc[:, half * 512:(half + 1) * 512], C.t_negrow[0:1, :],
                             carry[0:1, half * 512:(half + 1) * 512], False, True,
                             deps=[carry_tok], inc="Epe" if half == 1 else None)
                tclip = OP(C, "dve", lambda e, u=u: e.tensor_scalar(
                    A_sb[:, u, :], sc[:, :], 50.0, -60.0, ALU.min, ALU.max),
                    deps=[tmm, A_store, tA_prev, tsp])
                clip_toks.append(tclip)
                tcs = None
                for half in range(2):
                    tcs = MM(C, sc[0:1, half * 512:(half + 1) * 512], C.t_onescol[:, :],
                             g_sb[:, u, half * 512:(half + 1) * 512], True, True,
                             deps=[tclip] if half == 0 else (),
                             inc="Epe" if half == 1 else None)
                trw = OP(C, "act", lambda e: e.copy(rowt[0:1, :], sc[0:1, :]),
                         deps=[tcs, carry_tok])
                carry_tok = OP(C, "dve", lambda e: e.tensor_add(
                    carry[0:1, :], carry[0:1, :], rowt[0:1, :]), deps=[trw, carry_tok])
                sc_free = trw
                sp_free = tcs
                g_free = tcs

            tA_prev = OP(C, "act", lambda e: e.activation(A_sb[:, :, :], A_sb[:, :, :], AF.Exp),
                         deps=clip_toks)
            A_store = pl.dma("sp", C.v_A_t[s, tt], A_sb[:, :, :], "SA", deps=[tA_prev])
            p_store = pl.dma("sp", C.v_ph_t[s, tt], p_sb[:, :, :], "Sp", deps=pev + [M_tok])
            td.end()

        td.start()
        clk_store = pl.dma("sp", C.a_clock.ap()[s], clk_all[:, :], "Sclk", deps=[clk_tok])
        tpA = tp[0]
        tMc = OP(C, "act", lambda e: e.copy(bf1[:, :], M_sb[:, :]), deps=[M_tok, bf1_free])
        trM = None
        for ic in range(8):
            trM = TR(C, tpA[:, ic * 128:(ic + 1) * 128], bf1[:, ic * 128:(ic + 1) * 128],
                     deps=[tMc, tp_last[0]] if ic == 0 else (), inc="Epe" if ic == 7 else None)
        M_free = tMc
        bf1_free = trM
        red = []
        for ic in range(8):
            red.append(OP(C, "dve", lambda e, ic=ic, tpA=tpA: e.reduce_max(
                pmaxT[:, ic:ic + 1], tpA[:, ic * 128:(ic + 1) * 128], axis=AX.X),
                deps=[trM, pmaxT_free]))
        tst = pl.dma("sp", C.a_pmaxrow.ap()[s, :].rearrange("(c p) -> p c", p=128),
                     pmaxT[:, :], "Spm", deps=red, allow_slow_non_contiguous=True)
        pmaxT_free = tst
        tp_last[0] = red[-1]
        tld = pl.dma("sp", prow[0:1, :], C.a_pmaxrow.ap()[s:s + 1, :], "Dc", deps=[tst])
        MM(C, sc[:, 0:512], C.t_posrow[0:1, :], prow[0:1, 0:512], True, True,
           deps=[tld, sc_free])
        tb1 = MM(C, sc[:, 512:1024], C.t_posrow[0:1, :], prow[0:1, 512:1024], True, True,
                 deps=[tld], inc="Epe")
        tpm = OP(C, "act", lambda e, s=s: e.copy(C.pmax_b[s][:, :], sc[:, :]), deps=[tb1])
        sc_free = tpm
        td.end()

    pl.barrier()
    sc1.__exit__(None, None, None)

    sc2 = scope(nc); sc2.__enter__()
    w_q = sb((128, 8, D), BF16)
    w_k = sb((128, 8, D), BF16)
    w_v = sb((128, 8, D), BF16)
    tok_w2 = None
    for dst, src in [(w_q, C.i_wq), (w_k, C.i_wk), (w_v, C.i_wv)]:
        tok_w2 = pl.dma("pool", dst[:, :, :],
                        src.ap().rearrange("(c p) o -> p c o", p=128), "Dw")

    xn2T_in = [sb((128, 8, 512), BF16), sb((128, 8, 512), BF16)]
    A_in = [sb((128, D), F32), sb((128, D), F32)]
    ph_in = [sb((128, D), F32), sb((128, D), F32)]
    cs_in = [sb((128, 2, HALF), F32), sb((128, 2, HALF), F32)]
    clock_sb = sb((128, 32), F32)
    qa = sb((128, D), F32); ka = sb((128, D), F32)
    qr = sb((128, D), F32); kr = sb((128, D), F32)
    tq1 = sb((128, HALF), F32); tq2 = sb((128, HALF), F32)
    tk1 = sb((128, HALF), F32); tk2b = sb((128, HALF), F32)
    pex = sb((128, D), F32)
    pe_sb = sb((128, D), F32)
    pc_sb = sb((128, D), F32)
    rc_sb = sb((128, D), F32)
    t2a = sb((128, D), F32); t2r = sb((128, D), F32)
    qf1 = sb((128, D), F32); kk = sb((128, D), F32)
    qf_bf = sb((128, D), BF16); kf_bf = sb((128, D), BF16)
    qfT_st = sb((128, 8, 512), BF16); kfT_st = sb((128, 8, 512), BF16)
    v_st = sb((128, 4, D), BF16)
    st3 = sb((128, 12, 1), F32)

    td2 = TD(C, "T2")
    qfT_store = None; kfT_store = None; v_store = None
    pj0_free = None; pj1_free = None; sc_free2 = sc_free
    x2Tw = [None, None]
    qa_free = None; ka_free = None
    pe_free_pe = None; pe_free_pl = None
    pex_free = None; pc_free = None; t2r_free = None
    qf_free = None; kf_free = None
    clk_ld = None
    carry_tok2 = carry_tok

    def r2(ap3):
        return ap3.rearrange("p (h one) -> p h one", one=1)

    for s in range(2):
        clk_ld = pl.dma("sp", clock_sb[:, :], C.a_clock.ap()[s], "Dc", deps=[clk_store])
        carry_tok2 = OP(C, "dve", lambda e: e.memset(carry[0:1, :], 0.0), deps=[carry_tok2])
        for cc in range(32):
            td2.start()
            tt, u = cc // 4, cc % 4
            gj = s * 32 + cc
            cb = gj % 2
            b6 = 6 * cb
            if u == 0:
                x2Tw[tt % 2] = pl.dma("sp", xn2T_in[tt % 2][:, :, :],
                                      C.v_xn2T[s, :, :, tt * 512:(tt + 1) * 512], f"Dq{tt % 2}")
            xT = xn2T_in[tt % 2]
            tA_ = pl.dma("pool", A_in[cb][:, :], C.v_A_c[s, cc], f"Da{cb}")
            tP_ = pl.dma("pool", ph_in[cb][:, :], C.v_ph_c[s, cc], f"Dph{cb}")
            pl.dma("sp", cs_in[cb][:, 0, :], C.v_cos[cc], f"Dcs{cb}")
            tCS = pl.dma("sp", cs_in[cb][:, 1, :], C.v_sin[cc], f"Dcs{cb}")

            tmq = None
            for half in range(2):
                for ic in range(8):
                    tmq = MM(C, pj0[:, half * 512:(half + 1) * 512],
                             xT[:, ic, u * 128:(u + 1) * 128],
                             w_q[:, ic, half * 512:(half + 1) * 512],
                             start=(ic == 0), stop=(ic == 7),
                             deps=[x2Tw[tt % 2], tok_w2, pj0_free] if (ic == 0 and half == 0) else (),
                             inc="Epe" if (ic == 7 and half == 1) else None)
            teq = OP(C, "act", lambda e: e.copy(qa[:, :], pj0[:, :]), deps=[tmq, qa_free])
            pj0_free = teq
            tmk = None
            for half in range(2):
                for ic in range(8):
                    tmk = MM(C, pj1[:, half * 512:(half + 1) * 512],
                             xT[:, ic, u * 128:(u + 1) * 128],
                             w_k[:, ic, half * 512:(half + 1) * 512],
                             start=(ic == 0), stop=(ic == 7),
                             deps=[x2Tw[tt % 2], tok_w2, pj1_free] if (ic == 0 and half == 0) else (),
                             inc="Epe" if (ic == 7 and half == 1) else None)
            tek = OP(C, "act", lambda e: e.copy(ka[:, :], pj1[:, :]), deps=[tmk, ka_free])
            pj1_free = tek
            tmv = None
            for half in range(2):
                for ic in range(8):
                    tmv = MM(C, pj0[:, half * 512:(half + 1) * 512],
                             xT[:, ic, u * 128:(u + 1) * 128],
                             w_v[:, ic, half * 512:(half + 1) * 512],
                             start=(ic == 0), stop=(ic == 7),
                             deps=[teq] if (ic == 0 and half == 0) else (),
                             inc="Epe" if (ic == 7 and half == 1) else None)
            tev = OP(C, "act", lambda e, u=u: e.copy(v_st[:, u, :], pj0[:, :]),
                     deps=[tmv, v_store])
            pj0_free = tev

            tl2q = OP(C, "dve", lambda e, b6=b6: e.scalar_tensor_tensor(
                pex[:, :], qa[:, :], 1.0, qa[:, :], ALU.bypass, ALU.mult,
                accum_out=st3[:, b6 + 0:b6 + 1, 0]), deps=[teq, pex_free])
            tq_ = OP(C, "act", lambda e, b6=b6: e.activation(
                st3[:, b6 + 2:b6 + 3, :], st3[:, b6 + 0:b6 + 1, :], AF.Ln), deps=[tl2q])
            tq_ = OP(C, "act", lambda e, b6=b6: e.activation(
                st3[:, b6 + 2:b6 + 3, :], st3[:, b6 + 2:b6 + 3, :], AF.Exp, scale=-0.5), deps=[tq_])
            tqmin = OP(C, "dve", lambda e, b6=b6: e.tensor_scalar_min(
                st3[:, b6 + 4:b6 + 5, :], st3[:, b6 + 2:b6 + 3, :], 1.0e12), deps=[tq_])

            tl2k = OP(C, "dve", lambda e, b6=b6: e.scalar_tensor_tensor(
                kk[:, :], ka[:, :], 1.0, ka[:, :], ALU.bypass, ALU.mult,
                accum_out=st3[:, b6 + 1:b6 + 2, 0]), deps=[tek, kf_free])
            tk_ = OP(C, "act", lambda e, b6=b6: e.activation(
                st3[:, b6 + 3:b6 + 4, :], st3[:, b6 + 1:b6 + 2, :], AF.Ln), deps=[tl2k])
            tk_ = OP(C, "act", lambda e, b6=b6: e.activation(
                st3[:, b6 + 3:b6 + 4, :], st3[:, b6 + 3:b6 + 4, :], AF.Exp, scale=-0.5), deps=[tk_])
            tkmin = OP(C, "dve", lambda e, b6=b6: e.tensor_scalar_min(
                st3[:, b6 + 5:b6 + 6, :], st3[:, b6 + 3:b6 + 4, :], 1.0e12), deps=[tk_])

            CC = cs_in[cb][:, 0, :]
            SS = cs_in[cb][:, 1, :]
            t1 = OP(C, "dve", lambda e, CC=CC: e.tensor_mul(
                tq1[:, :], qa[:, 0:HALF], CC), deps=[teq, tCS])
            t2 = OP(C, "dve", lambda e, SS=SS: e.tensor_mul(
                tq2[:, :], qa[:, HALF:D], SS), deps=[teq, tCS])
            tqe = OP(C, "dve", lambda e: e.tensor_sub(
                qr[:, 0:HALF], tq1[:, :], tq2[:, :]), deps=[t1, t2])
            t1 = OP(C, "dve", lambda e, SS=SS: e.tensor_mul(
                tq1[:, :], qa[:, 0:HALF], SS), deps=[tqe])
            t2 = OP(C, "dve", lambda e, CC=CC: e.tensor_mul(
                tq2[:, :], qa[:, HALF:D], CC), deps=[tqe])
            tqo = OP(C, "dve", lambda e: e.tensor_add(
                qr[:, HALF:D], tq1[:, :], tq2[:, :]), deps=[t1, t2])
            qa_free = tqo

            k1 = OP(C, "pool", lambda e, CC=CC: e.tensor_mul(
                tk1[:, :], ka[:, 0:HALF], CC), deps=[tek, tCS])
            k2 = OP(C, "pool", lambda e, SS=SS: e.tensor_mul(
                tk2b[:, :], ka[:, HALF:D], SS), deps=[tek, tCS])
            tke = OP(C, "pool", lambda e: e.tensor_sub(
                kr[:, 0:HALF], tk1[:, :], tk2b[:, :]), deps=[k1, k2])
            k1 = OP(C, "pool", lambda e, SS=SS: e.tensor_mul(
                tk1[:, :], ka[:, 0:HALF], SS), deps=[tke])
            k2 = OP(C, "pool", lambda e, CC=CC: e.tensor_mul(
                tk2b[:, :], ka[:, HALF:D], CC), deps=[tke])
            tko = OP(C, "pool", lambda e: e.tensor_add(
                kr[:, HALF:D], tk1[:, :], tk2b[:, :]), deps=[k1, k2])
            ka_free = tko

            ts_ = OP(C, "dve", lambda e, cb=cb, s=s: e.tensor_sub(
                pex[:, :], ph_in[cb][:, :], C.pmax_b[s][:, :]), deps=[tP_, tl2q])
            pex_free = te_ = OP(C, "act", lambda e: e.activation(pe_sb[:, :], pex[:, :], AF.Exp),
                                deps=[ts_, pe_free_pe, pe_free_pl])
            tpe_ = OP(C, "dve", lambda e, cc=cc: e.tensor_scalar_mul(
                pe_sb[:, :], pe_sb[:, :], clock_sb[:, cc:cc + 1]), deps=[te_, clk_ld])

            tmm = None
            for half in range(2):
                tmm = MM(C, sc[:, half * 512:(half + 1) * 512], C.t_trip[:, :],
                         pe_sb[:, half * 512:(half + 1) * 512], True, False,
                         deps=[tpe_, sc_free2, carry_tok2] if half == 0 else ())
            for half in range(2):
                tmm = MM(C, sc[:, half * 512:(half + 1) * 512], C.t_posrow[0:1, :],
                         carry[0:1, half * 512:(half + 1) * 512], False, True,
                         deps=[carry_tok2], inc="Epe" if half == 1 else None)
            tpc = OP(C, "act", lambda e: e.activation(pc_sb[:, :], sc[:, :], AF.Ln,
                                                      bias=1e-8), deps=[tmm, pc_free, C.tok_consts])
            trc = OP(C, "act", lambda e: e.activation(rc_sb[:, :], pc_sb[:, :], AF.Exp,
                                                      scale=-1.0), deps=[tpc])
            pc_free = trc
            tcs = None
            for half in range(2):
                tcs = MM(C, sc[0:1, half * 512:(half + 1) * 512], C.t_onescol[:, :],
                         pe_sb[:, half * 512:(half + 1) * 512], True, True,
                         deps=[tpc] if half == 0 else (),
                         inc="Epe" if half == 1 else None)
            trw = OP(C, "act", lambda e: e.copy(rowt[0:1, :], sc[0:1, :]),
                     deps=[tcs, carry_tok2])
            carry_tok2 = OP(C, "dve", lambda e: e.tensor_add(
                carry[0:1, :], carry[0:1, :], rowt[0:1, :]), deps=[trw, carry_tok2])
            sc_free2 = trw
            pe_free_pe = tcs

            tq_ = OP(C, "dve", lambda e, cb=cb: e.scalar_tensor_tensor(
                qf1[:, :], rc_sb[:, :], SCALE, A_in[cb][:, :], ALU.mult, ALU.mult),
                deps=[trc, tA_])
            tqf = OP(C, "dve", lambda e, b6=b6: e.scalar_tensor_tensor(
                qf_bf[:, :], qr[:, :], st3[:, b6 + 4:b6 + 5, 0], qf1[:, :],
                ALU.mult, ALU.mult),
                deps=[tq_, tqmin, tqe, tqo, qf_free])
            ta_ = OP(C, "act", lambda e, cb=cb: e.activation(
                t2a[:, :], A_in[cb][:, :], AF.Ln, bias=1.0e-8),
                deps=[tA_, t2r_free, C.tok_consts])
            ta_ = OP(C, "act", lambda e: e.activation(
                t2r[:, :], t2a[:, :], AF.Exp, scale=-1.0), deps=[ta_])
            tkk = OP(C, "pool", lambda e: e.tensor_mul(kk[:, :], kr[:, :], pe_sb[:, :]),
                     deps=[tko, tpe_, tl2k])
            pe_free_pl = tkk
            tkf = OP(C, "dve", lambda e, b6=b6: e.scalar_tensor_tensor(
                kf_bf[:, :], kk[:, :], st3[:, b6 + 5:b6 + 6, 0], t2r[:, :],
                ALU.mult, ALU.mult),
                deps=[tkk, tkmin, ta_, kf_free])
            t2r_free = tkf

            tpA = tp[gj % 2]
            trq = None
            for ic in range(8):
                trq = TR(C, tpA[:, ic * 128:(ic + 1) * 128],
                         qf_bf[:, ic * 128:(ic + 1) * 128],
                         deps=[tqf, tp_last[gj % 2]] if ic == 0 else (),
                         inc="Epe" if ic == 7 else None)
            qf_free = trq
            evq = OP(C, "dve", lambda e, u=u, tpA=tpA: e.tensor_copy(
                qfT_st[:, :, u * 128:(u + 1) * 128],
                tpA[:, :].rearrange("p (c f) -> p c f", f=128)),
                deps=[trq, qfT_store])
            trk = None
            for ic in range(8):
                trk = TR(C, tpA[:, ic * 128:(ic + 1) * 128],
                         kf_bf[:, ic * 128:(ic + 1) * 128],
                         deps=[tkf, evq] if ic == 0 else (), inc="Epe" if ic == 7 else None)
            kf_free = trk
            evk = OP(C, "dve", lambda e, u=u, tpA=tpA: e.tensor_copy(
                kfT_st[:, :, u * 128:(u + 1) * 128],
                tpA[:, :].rearrange("p (c f) -> p c f", f=128)),
                deps=[trk, kfT_store])
            tp_last[gj % 2] = evk

            if u == 3:
                qfT_store = pl.dma("sp", C.v_qfT[s, :, :, tt * 512:(tt + 1) * 512],
                                   qfT_st[:, :, :], "Sqf", deps=[evq])
                kfT_store = pl.dma("sp", C.v_kfT[s, :, :, tt * 512:(tt + 1) * 512],
                                   kfT_st[:, :, :], "Skf", deps=[evk])
                v_store = pl.dma("sp", C.v_v_t[s, tt], v_st[:, :, :], "Sv", deps=[tev])
            td2.end()
    sc2.__exit__(None, None, None)


def plan_p3(C):
    nc, pl = C.nc, C.pl

    def sb(shape, dt):
        C._uid = getattr(C, "_uid", 0) + 1
        return nc.alloc_sbuf_tensor(f"sb{C._uid}", list(shape), dt)

    def ps(shape, dt):
        C._uid = getattr(C, "_uid", 0) + 1
        return nc.alloc_psum_tensor(f"ps{C._uid}", list(shape), dt)

    w_o = sb((128, 8, D), BF16)
    tok_wo = pl.dma("pool", w_o[:, :, :],
                    C.i_wo.ap().rearrange("(c p) o -> p c o", p=128), "Dw")
    r_ob = sb((1, D), F32)
    t_g2c = sb((128, 8), F32); t_b2c = sb((128, 8), F32)
    ob_b = sb((128, D), F32)
    tro = pl.dma("sp", r_ob[0:1, :], C.i_obrow.ap(), "Dc")
    trg = pl.dma("sp", t_g2c[:, :], C.i_g2c.ap(), "Dc")
    trb = pl.dma("sp", t_b2c[:, :], C.i_b2c.ap(), "Dc")

    s_ps = [ps((128, 512), F32), ps((128, 512), F32)]
    pt_ps = [ps((128, 512), BF16), ps((128, 512), BF16)]
    o_ps = [ps((128, D), F32), ps((128, D), F32)]

    tk_ob = bcast_row(C, r_ob, ob_b, o_ps[0], deps=[tro, C.tok_consts])

    qfT_blk = [[sb((128, 8, 512), BF16) for _ in range(2)] for _ in range(2)]
    kfT_g = [[sb((128, 8, 512), BF16) for _ in range(2)] for _ in range(2)]
    v_g = [[sb((128, 4, D), BF16) for _ in range(2)] for _ in range(2)]
    O = [[sb((128, D), F32) for _ in range(4)] for _ in range(2)]
    P_sb = [sb((128, 512), F32), sb((128, 512), F32)]
    Pb = [sb((128, 512), BF16), sb((128, 512), BF16)]
    ptb = [sb((128, 512), BF16), sb((128, 512), BF16)]
    nm = [sb((128, 4, 1), F32) for _ in range(2)]
    lsum = [sb((128, 4, 1), F32) for _ in range(2)]
    sml = sb((128, 16, 1), F32)
    xn_in = [sb((128, D), F32) for _ in range(2)]
    xnin_free = [None, None]
    at_bf = [sb((128, D), BF16), sb((128, D), BF16)]
    atT = [sb((128, 8, 128), BF16), sb((128, 8, 128), BF16)]
    x2a = [sb((128, D), F32), sb((128, D), F32)]
    x2b = [sb((128, D), F32), sb((128, D), F32)]
    hnb = [sb((128, D), BF16), sb((128, D), BF16)]
    hT_st = [sb((128, 8, 128), BF16), sb((128, 8, 128), BF16)]
    st4 = sb((128, 16, 1), F32)

    td = TD(C, "T3")
    O_tok = [[tk_ob] * 4 for _ in range(2)]
    nm_tok = [[None] * 4 for _ in range(2)]
    l_tok = [[None] * 4 for _ in range(2)]
    x2_store = [None, None]
    hT_store = [None, None]
    kvw = [[None, None], [None, None]]
    qw = [[None, None], [None, None]]
    first_pv_guard = tk_ob

    def emit_A(s, qb, jg, qt):
        if jg == 0 and qt == 0:
            qw[s][qb % 2] = pl.dma("sp" if s == 0 else "pool",
                                   qfT_blk[s][qb % 2][:, :, :],
                                   C.v_qfT[s, :, :, qb * 512:(qb + 1) * 512],
                                   f"Dq{s}{qb % 2}")
        if qt == 0:
            kvw[s][jg % 2] = pl.dma("sp" if s == 0 else "pool",
                                    kfT_g[s][jg % 2][:, :, :],
                                    C.v_kfT[s, :, :, jg * 512:(jg + 1) * 512],
                                    f"Dkv{s}{jg % 2}")
            kvw[s][jg % 2] = pl.dma("sp" if s == 0 else "pool",
                                    v_g[s][jg % 2][:, :, :], C.v_v_g[s, jg],
                                    f"Dkv{s}{jg % 2}")
        r = s
        sp_ = s_ps[r]
        tm = None
        for ic in range(8):
            tm = MM(C, sp_[:, :], qfT_blk[s][qb % 2][:, ic, qt * 128:(qt + 1) * 128],
                    kfT_g[s][jg % 2][:, ic, :], start=(ic == 0), stop=(ic == 7),
                    deps=[qw[s][qb % 2], kvw[s][jg % 2]] if ic == 0 else (),
                    inc="Epe" if ic == 7 else None)
        if jg == qb:
            for j2 in range(4):
                kt = jg * 4 + j2
                qtg = qb * 4 + qt
                if kt == qtg:
                    tm = OP(C, "dve", lambda e, sp_=sp_, j2=j2: e.tensor_add(
                        sp_[:, j2 * 128:(j2 + 1) * 128],
                        sp_[:, j2 * 128:(j2 + 1) * 128], C.t_mask[:, :]),
                        deps=[tm, C.tok_consts])
                elif kt > qtg:
                    tm = OP(C, "dve", lambda e, sp_=sp_, j2=j2: e.tensor_scalar_add(
                        sp_[:, j2 * 128:(j2 + 1) * 128],
                        sp_[:, j2 * 128:(j2 + 1) * 128], NEG), deps=[tm])
        tmx = OP(C, "dve", lambda e, sp_=sp_, r=r: e.reduce_max(
            sml[:, r:r + 1, :], sp_[:, :], axis=AX.X, negate=True), deps=[tm])
        if jg == 0:
            nm_tok[s][qt] = OP(C, "dve", lambda e, s=s, qt=qt, r=r: e.tensor_copy(
                nm[s][:, qt:qt + 1, :], sml[:, r:r + 1, :]), deps=[tmx, nm_tok[s][qt]])
            texp = OP(C, "act", lambda e, s=s, sp_=sp_, r=r, qt=qt: e.activation(
                P_sb[r][:, :], sp_[:, :], AF.Exp, bias=nm[s][:, qt:qt + 1, 0],
                accum_out=lsum[s][:, qt:qt + 1, 0]),
                deps=[nm_tok[s][qt], l_tok[s][qt]])
            l_tok[s][qt] = texp
            tal = None
        else:
            tmin = OP(C, "dve", lambda e, s=s, qt=qt, r=r: e.tensor_tensor(
                sml[:, 2 + r:3 + r, :], nm[s][:, qt:qt + 1, :], sml[:, r:r + 1, :],
                ALU.min), deps=[tmx, nm_tok[s][qt]])
            tda = OP(C, "dve", lambda e, s=s, qt=qt, r=r: e.tensor_sub(
                sml[:, 4 + r:5 + r, :], sml[:, 2 + r:3 + r, :], nm[s][:, qt:qt + 1, :]),
                deps=[tmin])
            tal = OP(C, "act", lambda e, r=r: e.activation(
                sml[:, 6 + r:7 + r, :], sml[:, 4 + r:5 + r, :], AF.Exp), deps=[tda])
            nm_tok[s][qt] = OP(C, "dve", lambda e, s=s, qt=qt, r=r: e.tensor_copy(
                nm[s][:, qt:qt + 1, :], sml[:, 2 + r:3 + r, :]), deps=[tda])
            texp = OP(C, "act", lambda e, s=s, sp_=sp_, r=r, qt=qt: e.activation(
                P_sb[r][:, :], sp_[:, :], AF.Exp, bias=nm[s][:, qt:qt + 1, 0],
                accum_out=sml[:, 8 + r:9 + r, 0]),
                deps=[nm_tok[s][qt], l_tok[s][qt]])
            l_tok[s][qt] = OP(C, "dve", lambda e, s=s, qt=qt, r=r: e.scalar_tensor_tensor(
                lsum[s][:, qt:qt + 1, :], lsum[s][:, qt:qt + 1, :], sml[:, 6 + r:7 + r, 0],
                sml[:, 8 + r:9 + r, :], ALU.mult, ALU.add),
                deps=[texp, tal, l_tok[s][qt]])
        tcast = OP(C, "dve", lambda e, r=r: e.tensor_copy(Pb[r][:, :], P_sb[r][:, :]),
                   deps=[texp])
        return dict(kind="m", s=s, r=r, jg=jg, qt=qt, tcast=tcast, tal=tal)

    def emit_B(cx):
        nonlocal first_pv_guard
        if cx is None or cx["kind"] != "m":
            return
        s, r, jg, qt, tal = cx["s"], cx["r"], cx["jg"], cx["qt"], cx["tal"]
        ptp = pt_ps[r]; op_ = o_ps[r]
        trp = None
        for j2 in range(4):
            trp = TR(C, ptp[:, j2 * 128:(j2 + 1) * 128],
                     Pb[r][:, j2 * 128:(j2 + 1) * 128],
                     deps=[cx["tcast"]] if j2 == 0 else (),
                     inc="Epe" if j2 == 3 else None)
        tev = OP(C, "dve", lambda e, r=r, ptp=ptp: e.tensor_copy(
            ptb[r][:, :], ptp[:, :]), deps=[trp])
        tpv = None
        for half in range(2):
            for j2 in range(4):
                tpv = MM(C, op_[:, half * 512:(half + 1) * 512],
                         ptb[r][:, j2 * 128:(j2 + 1) * 128],
                         v_g[s][jg % 2][:, j2, half * 512:(half + 1) * 512],
                         start=(j2 == 0), stop=(j2 == 3),
                         deps=[tev, first_pv_guard] if (j2 == 0 and half == 0) else (),
                         inc="Epe" if (j2 == 3 and half == 1) else None)
        first_pv_guard = None
        if jg == 0:
            O_tok[s][qt] = OP(C, "act", lambda e, s=s, qt=qt, op_=op_: e.copy(
                O[s][qt][:, :], op_[:, :]), deps=[tpv, O_tok[s][qt]])
        else:
            O_tok[s][qt] = OP(C, "dve", lambda e, s=s, qt=qt, op_=op_, r=r: e.scalar_tensor_tensor(
                O[s][qt][:, :], O[s][qt][:, :], sml[:, 6 + r:7 + r, 0], op_[:, :],
                ALU.mult, ALU.add), deps=[tpv, tal, O_tok[s][qt]])

    def emit_E(s, qb, qt):
        r = s
        qtile = qb * 4 + qt
        op_ = o_ps[r]; ptp = pt_ps[r]
        p8 = 8 * r
        xl = pl.dma("sp" if s == 0 else "pool", xn_in[s][:, :],
                    C.v_xn_q[s, qtile], f"Dx{s}", deps=[xnin_free[s]])
        tinv = OP(C, "dve", lambda e, s=s, qt=qt, r=r: e.reciprocal(
            sml[:, 10 + r:11 + r, :], lsum[s][:, qt:qt + 1, :]), deps=[l_tok[s][qt]])
        l_tok[s][qt] = tinv
        tab = OP(C, "dve", lambda e, s=s, qt=qt, r=r: e.tensor_scalar_mul(
            at_bf[r][:, :], O[s][qt][:, :], sml[:, 10 + r:11 + r, 0]),
            deps=[tinv, O_tok[s][qt]])
        O_tok[s][qt] = tab
        evs = []
        for rnd in range(2):
            trp = None
            for j in range(4):
                trp = TR(C, ptp[:, j * 128:(j + 1) * 128],
                         at_bf[r][:, (rnd * 4 + j) * 128:(rnd * 4 + j + 1) * 128],
                         deps=([tab] if rnd == 0 else [evs[0]]) if j == 0 else (),
                         inc="Epe" if j == 3 else None)
            evs.append(OP(C, "dve", lambda e, rnd=rnd, r=r, ptp=ptp: e.tensor_copy(
                atT[r][:, rnd * 4:(rnd + 1) * 4, :],
                ptp[:, :].rearrange("p (c f) -> p c f", f=128)), deps=[trp]))
        tmo = None
        for half in range(2):
            for ic in range(8):
                tmo = MM(C, op_[:, half * 512:(half + 1) * 512],
                         atT[r][:, ic, :], w_o[:, ic, half * 512:(half + 1) * 512],
                         start=(ic == 0), stop=(ic == 7),
                         deps=[evs[1], tok_wo] if (ic == 0 and half == 0) else (),
                         inc="Epe" if (ic == 7 and half == 1) else None)
        t1 = OP(C, "dve", lambda e, s=s, op_=op_, r=r: e.scalar_tensor_tensor(
            x2a[r][:, :], xn_in[s][:, :], 1.0, op_[:, :], ALU.bypass, ALU.add),
            deps=[tmo, xl])
        xnin_free[s] = t1
        t2 = OP(C, "pool", lambda e, r=r: e.tensor_add(
            x2b[r][:, :], x2a[r][:, :], ob_b[:, :]),
            deps=[t1, tk_ob, x2_store[r]])
        x2_store[r] = pl.dma("sp", C.v_x2_q[s, qtile], x2b[r][:, :], f"Sx2{r}", deps=[t2])

        t_s1 = OP(C, "dve", lambda e, r=r, p8=p8: e.reduce_sum(
            st4[:, p8 + 0:p8 + 1, :], x2b[r][:, :], axis=AX.X), deps=[t2])
        t_sq = OP(C, "act", lambda e, r=r: e.activation(
            x2a[r][:, :], x2b[r][:, :], AF.Square), deps=[t2, t1])
        t_ssq = OP(C, "dve", lambda e, r=r, p8=p8: e.reduce_sum(
            st4[:, p8 + 1:p8 + 2, :], x2a[r][:, :], axis=AX.X), deps=[t_sq])
        t_m = OP(C, "dve", lambda e, p8=p8: e.tensor_scalar_mul(
            st4[:, p8 + 2:p8 + 3, :], st4[:, p8 + 0:p8 + 1, :], 1.0 / D), deps=[t_s1])
        t_v = OP(C, "dve", lambda e, p8=p8: e.tensor_scalar_mul(
            st4[:, p8 + 3:p8 + 4, :], st4[:, p8 + 1:p8 + 2, :], 1.0 / D), deps=[t_ssq])
        t_mm = OP(C, "dve", lambda e, p8=p8: e.tensor_mul(
            st4[:, p8 + 4:p8 + 5, :], st4[:, p8 + 2:p8 + 3, :], st4[:, p8 + 2:p8 + 3, :]),
            deps=[t_m])
        t_v = OP(C, "dve", lambda e, p8=p8: e.tensor_sub(
            st4[:, p8 + 3:p8 + 4, :], st4[:, p8 + 3:p8 + 4, :], st4[:, p8 + 4:p8 + 5, :]),
            deps=[t_v, t_mm])
        t_l = OP(C, "act", lambda e, p8=p8: e.activation(
            st4[:, p8 + 3:p8 + 4, :], st4[:, p8 + 3:p8 + 4, :], AF.Ln, bias=1e-5),
            deps=[t_v, C.tok_consts])
        t_r = OP(C, "act", lambda e, p8=p8: e.activation(
            st4[:, p8 + 5:p8 + 6, :], st4[:, p8 + 3:p8 + 4, :], AF.Exp, scale=-0.5),
            deps=[t_l])
        thn = OP(C, "dve", lambda e, r=r, p8=p8: e.tensor_scalar(
            hnb[r][:, :], x2b[r][:, :], st4[:, p8 + 2:p8 + 3, :], st4[:, p8 + 5:p8 + 6, :],
            ALU.subtract, ALU.mult), deps=[t_r])
        hev = []
        for rnd in range(2):
            trp = None
            for j in range(4):
                trp = TR(C, ptp[:, j * 128:(j + 1) * 128],
                         hnb[r][:, (rnd * 4 + j) * 128:(rnd * 4 + j + 1) * 128],
                         deps=([thn, tmo, evs[1]] if rnd == 0 else [hev[3]]) if j == 0 else (),
                         inc="Epe" if j == 3 else None)
            for ic4 in range(4):
                ic = rnd * 4 + ic4
                hev.append(OP(C, "dve", lambda e, r=r, ic=ic, ic4=ic4, ptp=ptp: e.tensor_scalar(
                    hT_st[r][:, ic, :], ptp[:, ic4 * 128:(ic4 + 1) * 128],
                    t_g2c[:, ic:ic + 1], t_b2c[:, ic:ic + 1], ALU.mult, ALU.add),
                    deps=[trp, hT_store[r], trg, trb]))
        hT_store[r] = pl.dma("sp", C.v_hT[s, :, :, qtile * 128:(qtile + 1) * 128],
                             hT_st[r][:, :, :], f"ShT{r}", deps=hev)
        return dict(kind="e")

    def stream_steps(s):
        out = []
        for qb in range(8):
            for jg in range(qb + 1):
                for qt in range(4):
                    out.append(("m", s, qb, jg, qt))
            for qt in range(4):
                out.append(("e", s, qb, None, qt))
        return out

    st0, st1 = stream_steps(0), stream_steps(1)
    glob = []
    for i in range(len(st0)):
        glob.append(st0[i])
        glob.append(st1[i])

    prev_cx = None
    for (kind, s, qb, jg, qt) in glob:
        td.start()
        cx = emit_A(s, qb, jg, qt) if kind == "m" else emit_E(s, qb, qt)
        emit_B(prev_cx)
        prev_cx = cx
        td.end()
    td.start()
    emit_B(prev_cx)
    td.end()


def plan_p4(C):
    nc, pl = C.nc, C.pl

    def sb(shape, dt):
        C._uid = getattr(C, "_uid", 0) + 1
        return nc.alloc_sbuf_tensor(f"sb{C._uid}", list(shape), dt)

    def ps(shape, dt):
        C._uid = getattr(C, "_uid", 0) + 1
        return nc.alloc_psum_tensor(f"ps{C._uid}", list(shape), dt)

    w1 = sb((128, 8, 32, 128), BF16)
    w2 = sb((128, 32, D), BF16)
    b1 = sb((128, 32), F32)
    b2_b = sb((128, D), F32)
    u_ps = [ps((128, 512), F32), ps((128, 512), F32)]
    y_ps = [ps((128, D), F32), ps((128, D), F32)]

    tok_w1 = pl.dma("pool", w1[:, :, :, :],
                    C.i_w1m.ap().rearrange("(c p) (m f) -> p c m f", p=128, f=128), "Dw")
    tok_w2 = pl.dma("pool", w2[:, :, :],
                    C.i_w2m.ap().rearrange("(m p) o -> p m o", p=128), "Dw")
    tok_b1 = pl.dma("sp", b1[:, :], C.i_b1m_pc.ap(), "Dc")
    scb2 = scope(nc); scb2.__enter__()
    r_b2 = sb((1, D), F32)
    trb2 = pl.dma("sp", r_b2[0:1, :], C.i_b2mrow.ap(), "Dc")
    tk_b2 = bcast_row(C, r_b2, b2_b, y_ps[0], deps=[trb2, C.tok_consts])
    scb2.__exit__(None, None, None)

    hT_in = [sb((128, 8, 512), BF16), sb((128, 8, 512), BF16)]
    g_st = sb((128, 32, 512), BF16)
    x2_in = [sb((128, D), F32), sb((128, D), F32)]
    x3_st = [sb((128, D), F32)]

    td = TD(C, "T4")
    g_free = tok_w1
    u_free = [None, None]
    y_free = [tk_b2, tk_b2]
    x2_used = [None, None]
    x3_store = [None]

    for i in range(16):
        s, tt = i // 8, i % 8
        td.start()
        th = pl.dma("sp", hT_in[i % 2][:, :, :],
                    C.v_hT[s, :, :, tt * 512:(tt + 1) * 512], f"Dq{i % 2}",
                    deps=[tk_b2] if i == 0 else ())
        gel = []
        for fc in range(32):
            tu = None
            for ic in range(8):
                tu = MM(C, u_ps[fc % 2][:, :], w1[:, ic, fc, :],
                        hT_in[i % 2][:, ic, :], start=(ic == 0), stop=(ic == 7),
                        deps=[th, tok_w1, u_free[fc % 2], g_free] if ic == 0 else (),
                        inc="Epe" if ic == 7 else None)
            tg = OP(C, "act", lambda e, fc=fc: e.activation(
                g_st[:, fc, :], u_ps[fc % 2][:, :], AF.Gelu, bias=b1[:, fc:fc + 1]),
                deps=[tu, tok_b1])
            u_free[fc % 2] = tg
            gel.append(tg)
        last_mm = None
        for ts4 in range(4):
            q = tt * 4 + ts4
            x2l = pl.dma("pool", x2_in[ts4 % 2][:, :], C.v_x2_q[s, q], f"Dx{ts4 % 2}",
                         deps=[x2_used[ts4 % 2]])
            ty = None
            for half in range(2):
                for fc in range(32):
                    ty = MM(C, y_ps[ts4 % 2][:, half * 512:(half + 1) * 512],
                            g_st[:, fc, ts4 * 128:(ts4 + 1) * 128],
                            w2[:, fc, half * 512:(half + 1) * 512],
                            start=(fc == 0), stop=(fc == 31),
                            deps=[gel[-1], tok_w2, y_free[ts4 % 2]] if (fc == 0 and half == 0) else (),
                            inc="Epe" if (fc == 31 and half == 1) else None)
            last_mm = ty
            t1 = OP(C, "dve", lambda e, ts4=ts4: e.scalar_tensor_tensor(
                x3_st[0][:, :], x2_in[ts4 % 2][:, :], 1.0, y_ps[ts4 % 2][:, :],
                ALU.bypass, ALU.add), deps=[ty, x2l, x3_store[0]])
            x2_used[ts4 % 2] = t1
            y_free[ts4 % 2] = t1
            t2 = OP(C, "dve", lambda e: e.tensor_add(
                x3_st[0][:, :], x3_st[0][:, :], b2_b[:, :]), deps=[t1, tk_b2])
            x3_store[0] = pl.dma("sp", C.v_x3_q[s, q], x3_st[0][:, :],
                                 "Sx3", deps=[t2])
        g_free = last_mm
        td.end()


def plan_p5(C):
    nc, pl = C.nc, C.pl

    def sb(shape, dt):
        C._uid = getattr(C, "_uid", 0) + 1
        return nc.alloc_sbuf_tensor(f"sb{C._uid}", list(shape), dt)

    def ps(shape, dt):
        C._uid = getattr(C, "_uid", 0) + 1
        return nc.alloc_psum_tensor(f"ps{C._uid}", list(shape), dt)

    w1t = sb((128, 8, 32, 128), BF16)
    w2t = sb((128, 8, 32, 128), BF16)
    s1c = sb((128, 32), F32)
    s2c = sb((128, 32), F32)
    u1 = [ps((128, 512), F32), ps((128, 512), F32)]
    u2 = [ps((128, 512), F32), ps((128, 512), F32)]

    tok_w1 = pl.dma("pool", w1t[:, :, :, :],
                    C.i_fw1t.ap().rearrange("(c p) (m f) -> p c m f", p=128, f=128), "Dw")
    tok_w2 = pl.dma("pool", w2t[:, :, :, :],
                    C.i_fw2t.ap().rearrange("(c p) (m f) -> p c m f", p=128, f=128), "Dw")
    tok_s1 = pl.dma("sp", s1c[:, :], C.i_s1_pc.ap(), "Dc")
    tok_s2 = pl.dma("sp", s2c[:, :], C.i_s2_pc.ap(), "Dc")

    xnT_in = [sb((128, 8, 512), BF16), sb((128, 8, 512), BF16)]
    h_st = sb((128, 32, 512), BF16)
    t1f = [sb((128, 512), F32), sb((128, 512), F32)]
    t2f = [sb((128, 512), F32), sb((128, 512), F32)]

    td = TD(C, "T5")
    u1_free = [None, None]
    u2_free = [None, None]
    h_free = None
    t1_free = [None, None]
    t2_free = [None, None]

    for i in range(16):
        s, tt = i // 8, i % 8
        td.start()
        tx = pl.dma("sp", xnT_in[i % 2][:, :, :],
                    C.v_xnT[s, :, :, tt * 512:(tt + 1) * 512], f"Dq{i % 2}")
        evs = []
        for oc in range(32):
            tu1 = None
            for ic in range(8):
                tu1 = MM(C, u1[oc % 2][:, :], w1t[:, ic, oc, :], xnT_in[i % 2][:, ic, :],
                         start=(ic == 0), stop=(ic == 7),
                         deps=[tx, tok_w1, u1_free[oc % 2]] if ic == 0 else (),
                         inc="Epe" if ic == 7 else None)
            tu2 = None
            for ic in range(8):
                tu2 = MM(C, u2[oc % 2][:, :], w2t[:, ic, oc, :], xnT_in[i % 2][:, ic, :],
                         start=(ic == 0), stop=(ic == 7),
                         deps=[tok_w2, u2_free[oc % 2]] if ic == 0 else (),
                         inc="Epe" if ic == 7 else None)
            ts1 = OP(C, "act", lambda e, oc=oc: e.activation(
                t1f[oc % 2][:, :], u1[oc % 2][:, :], AF.Silu, scale=s1c[:, oc:oc + 1]),
                deps=[tu1, tok_s1, t1_free[oc % 2]])
            u1_free[oc % 2] = ts1
            ts2 = OP(C, "dve", lambda e, oc=oc: e.tensor_scalar_mul(
                t2f[oc % 2][:, :], u2[oc % 2][:, :], s2c[:, oc:oc + 1]),
                deps=[tu2, tok_s2, t2_free[oc % 2]])
            u2_free[oc % 2] = ts2
            tm = OP(C, "dve", lambda e, oc=oc: e.tensor_mul(
                h_st[:, oc, :], t1f[oc % 2][:, :], t2f[oc % 2][:, :]),
                deps=[ts1, ts2, h_free])
            t1_free[oc % 2] = tm
            t2_free[oc % 2] = tm
            evs.append(tm)
        h_free = pl.dma("sp", C.v_h[s, tt // 2, :, :, (tt % 2) * 512:(tt % 2) * 512 + 512],
                        h_st[:, :, :], "Sh", deps=evs)
        td.end()


def plan_p67(C, layer):
    nc, pl = C.nc, C.pl

    def sb(shape, dt):
        C._uid = getattr(C, "_uid", 0) + 1
        return nc.alloc_sbuf_tensor(f"sb{C._uid}", list(shape), dt)

    def ps(shape, dt):
        C._uid = getattr(C, "_uid", 0) + 1
        return nc.alloc_psum_tensor(f"ps{C._uid}", list(shape), dt)

    srcv = C.v_h if layer == 1 else C.v_g1
    dstv = C.v_g1 if layer == 1 else C.v_g2
    wT = C.d_h1R if layer == 1 else C.d_h2R
    scin = C.i_h1s_pc if layer == 1 else C.i_h2s_pc

    sb((128, H4), BF16)
    hin = [sb((128, 32, 1024), BF16), sb((128, 32, 1024), BF16)]
    wsl = [sb((128, H4), BF16) for _ in range(3)]
    go = [sb((128, 512), BF16), sb((128, 512), BF16)]
    sc_ = sb((128, 32), F32)
    g_ps = [ps((128, 512), F32) for _ in range(4)]

    tok_sc = pl.dma("sp", sc_[:, :], scin.ap(), "Dc")
    wv = wT.ap()

    td = TD(C, "T6" if layer == 1 else "T7")
    gfree = [None] * 4
    go_store = [None, None]
    thi = [None, None]
    NU = 256
    WQ = [("pool", 0, 2048), ("act", 2048, 3072), ("sp", 3072, 4096)]

    def load_wsl(u, dep):
        toks = []
        for qn, a, b in WQ:
            toks.append(pl.dma(
                qn, wsl[u % 3][:, a:b], wv[u % 32][:, a:b], f"Dwl{qn}",
                deps=[dep] if dep is not None else ()))
        return toks

    thi[0] = pl.dma("pool", hin[0][:, :, :], srcv[0, 0], "Dh0")
    wtoks = {0: load_wsl(0, None), 1: load_wsl(1, None)}
    mm_last = {}
    for g in range(8):
        s, off = g // 4, (g % 4) * 1024
        for oc in range(32):
            u = g * 32 + oc
            td.start()
            if u + 2 < NU:
                wtoks[u + 2] = load_wsl(u + 2, mm_last.get(u - 1))
            if oc == 1 and g + 1 < 8:
                sn = (g + 1) // 4
                thi[(g + 1) % 2] = pl.dma(
                    "pool", hin[(g + 1) % 2][:, :, :],
                    srcv[sn, (g + 1) % 4], f"Dh{(g + 1) % 2}")
            for half in range(2):
                idx = (oc * 2 + half) % 4
                tm = None
                for ic in range(32):
                    tm = MM(C, g_ps[idx][:, :], wsl[u % 3][:, ic * 128:(ic + 1) * 128],
                            hin[g % 2][:, ic, half * 512:(half + 1) * 512],
                            start=(ic == 0), stop=(ic == 31),
                            deps=(wtoks[u] + [thi[g % 2], gfree[idx]]) if ic == 0 else (),
                            inc="Epe" if ic == 31 else None)
                tg = OP(C, "act", lambda e, half=half, idx=idx, oc=oc: e.activation(
                    go[half][:, :], g_ps[idx][:, :], AF.Gelu, scale=sc_[:, oc:oc + 1]),
                    deps=[tm, tok_sc, go_store[half]])
                gfree[idx] = tg
                go_store[half] = pl.dma(
                    "sp", dstv[s, g % 4, :, oc, half * 512:(half + 1) * 512],
                    go[half][:, :], f"Sg{half}", deps=[tg])
            mm_last[u] = tm
            td.end()


def plan_p8(C):
    nc, pl = C.nc, C.pl

    def sb(shape, dt):
        C._uid = getattr(C, "_uid", 0) + 1
        return nc.alloc_sbuf_tensor(f"sb{C._uid}", list(shape), dt)

    def ps(shape, dt):
        C._uid = getattr(C, "_uid", 0) + 1
        return nc.alloc_psum_tensor(f"ps{C._uid}", list(shape), dt)

    w3 = sb((128, 32, D), BF16)
    s3r = sb((1, D), F32)
    s3b = sb((128, D), F32)
    y_ps = [ps((128, D), F32), ps((128, D), F32)]

    tok_w3 = pl.dma("pool", w3[:, :, :],
                    C.i_fw3t.ap().rearrange("(m p) o -> p m o", p=128), "Dw")
    trs3 = pl.dma("sp", s3r[0:1, :], C.i_s3row.ap(), "Dc")
    tk_s3 = bcast_row(C, s3r, s3b, y_ps[0], deps=[trs3, C.tok_consts])

    g2in = [sb((128, 32, 512), BF16), sb((128, 32, 512), BF16)]
    x3in = [sb((128, D), F32), sb((128, D), F32)]
    xin = [sb((128, D), F32), sb((128, D), F32)]
    tmpa = [sb((128, D), F32), sb((128, D), F32)]
    outst = sb((128, 4, D), F32)

    td = TD(C, "T8")
    y_free = [tk_s3, tk_s3]
    x3_used = [None, None]
    x_used = [None, None]
    tmp_used = [None, None]
    out_store = None

    for i in range(16):
        s, tt = i // 8, i % 8
        td.start()
        tg = pl.dma("sp", g2in[i % 2][:, :, :],
                    C.v_g2[s, tt // 2, :, :, (tt % 2) * 512:(tt % 2) * 512 + 512],
                    f"Dq{i % 2}")
        evs = []
        for ts4 in range(4):
            q = tt * 4 + ts4
            tx3 = pl.dma("pool", x3in[ts4 % 2][:, :], C.v_x3_q[s, q], f"Dx{ts4 % 2}",
                         deps=[x3_used[ts4 % 2]])
            tx = pl.dma("pool", xin[ts4 % 2][:, :], C.v_x_q[s, q], f"Dx2{ts4 % 2}",
                        deps=[x_used[ts4 % 2]])
            ty = None
            for half in range(2):
                for fc in range(32):
                    ty = MM(C, y_ps[ts4 % 2][:, half * 512:(half + 1) * 512],
                            g2in[i % 2][:, fc, ts4 * 128:(ts4 + 1) * 128],
                            w3[:, fc, half * 512:(half + 1) * 512],
                            start=(fc == 0), stop=(fc == 31),
                            deps=[tg, tok_w3, y_free[ts4 % 2]] if (fc == 0 and half == 0) else (),
                            inc="Epe" if (fc == 31 and half == 1) else None)
            t1 = OP(C, "dve", lambda e, ts4=ts4: e.scalar_tensor_tensor(
                tmpa[ts4 % 2][:, :], s3b[:, :], 1.0, y_ps[ts4 % 2][:, :],
                ALU.bypass, ALU.mult), deps=[ty, tk_s3, tmp_used[ts4 % 2]])
            y_free[ts4 % 2] = t1
            t2 = OP(C, "dve", lambda e, ts4=ts4: e.tensor_add(
                tmpa[ts4 % 2][:, :], tmpa[ts4 % 2][:, :], x3in[ts4 % 2][:, :]),
                deps=[t1, tx3])
            x3_used[ts4 % 2] = t2
            t3 = OP(C, "dve", lambda e, ts4=ts4: e.tensor_add(
                outst[:, ts4, :], tmpa[ts4 % 2][:, :], xin[ts4 % 2][:, :]),
                deps=[t2, tx, out_store])
            x_used[ts4 % 2] = t3
            tmp_used[ts4 % 2] = t3
            evs.append(t3)
        out_store = pl.dma("sp", C.v_out_t[s, tt], outst[:, :, :], "Sout", deps=evs)
        td.end()


def _host_consts():
    i = np.arange(128)
    tri = (i[:, None] <= i[None, :]).astype(np.float32)
    consts = {
        "ctri_n": -tri,
        "ctri_p": tri,
        "cnegrow": -np.ones((1, 128), np.float32),
        "cposrow": np.ones((1, 128), np.float32),
        "conescol": np.ones((128, 1), np.float32),
        "cident": np.eye(128, dtype=BF),
        "cmasktri": np.where(i[None, :] <= i[:, None], 0.0, NEG).astype(np.float32),
        "cbias": np.broadcast_to(np.array([1e-5, 1.0, 1e-8], np.float32), (128, 3)).copy(),
    }
    freqs = np.exp(-np.arange(HALF, dtype=np.float32) * (math.log(10000.0) / HALF))
    th = np.arange(T, dtype=np.float32)[:, None] * freqs[None, :]
    consts["ccos"] = np.cos(th).astype(np.float32)
    consts["csin"] = np.sin(th).astype(np.float32)
    return consts


def _layout_inputs(inputs):
    f = np.ascontiguousarray
    perm = np.concatenate([np.arange(0, D, 2), np.arange(1, D, 2)])
    d = {
        "wq": inputs["L0_attn_Wq"][:, perm], "wk": inputs["L0_attn_Wk"][:, perm],
        "wv": inputs["L0_attn_Wv"],
        "wgate": inputs["L0_attn_gate_W"][:, perm], "wp": inputs["L0_attn_p_W"][:, perm],
        "wclock": inputs["L0_attn_clock_W"], "wo": inputs["L0_attn_out_W"],
        "w1m": inputs["L0_attn_mlp_W1"], "w2m": inputs["L0_attn_mlp_W2"],
        "fw1t": f(inputs["L0_ffn_w1_w"].T), "fw2t": f(inputs["L0_ffn_w2_w"].T),
        "fh1r": f(inputs["L0_ffn_h1_w"].T.reshape(32, 128, 32, 128).transpose(2, 1, 0, 3)),
        "fh2r": f(inputs["L0_ffn_h2_w"].T.reshape(32, 128, 32, 128).transpose(2, 1, 0, 3)),
        "fw3t": f(inputs["L0_ffn_w3_w"].T),
        "gbrow": inputs["L0_attn_gate_b"][perm].reshape(1, D),
        "pbrow": inputs["L0_attn_p_b"][perm].reshape(1, D),
        "obrow": inputs["L0_attn_out_b"].reshape(1, D),
        "b2mrow": inputs["L0_attn_mlp_b2"].reshape(1, D),
        "s3row": inputs["L0_ffn_w3_s"].reshape(1, D),
        "cbrow": inputs["L0_attn_clock_b"].reshape(1, 1),
        "g1c": f(inputs["L0_attn_ln1_g"].reshape(8, 128).T),
        "b1c": f(inputs["L0_attn_ln1_b"].reshape(8, 128).T),
        "g2c": f(inputs["L0_attn_ln2_g"].reshape(8, 128).T),
        "b2c": f(inputs["L0_attn_ln2_b"].reshape(8, 128).T),
        "b1m_pc": f(inputs["L0_attn_mlp_b1"].reshape(32, 128).T),
        "s1_pc": f(inputs["L0_ffn_w1_s"].reshape(32, 128).T),
        "s2_pc": f(inputs["L0_ffn_w2_s"].reshape(32, 128).T),
        "h1s_pc": f(inputs["L0_ffn_h1_s"].reshape(32, 128).T),
        "h2s_pc": f(inputs["L0_ffn_h2_s"].reshape(32, 128).T),
    }
    return {k: np.ascontiguousarray(np.asarray(v, dtype=np.float32)) for k, v in d.items()}


_PASSES = {"p0", "p1", "p3", "p4", "p5", "p6", "p8"}


def kernel(**inputs):
    cfg = SimpleNamespace(passes=_PASSES)
    nc, C = build_nc(cfg)
    x = np.asarray(inputs["L0_attn_x"], dtype=np.float32)
    shared = _layout_inputs(inputs)
    shared.update(_host_consts())
    in_maps = []
    for c in range(NCORES):
        m = dict(shared)
        m["x"] = np.ascontiguousarray(x[c * 2:(c + 1) * 2])
        in_maps.append(m)
    res = run_bass_kernel_spmd(nc, in_maps, core_ids=list(range(NCORES)))
    return np.concatenate([r["out"] for r in res.results], axis=0)


if __name__ == "__main__":
    import sys
    passes = set(sys.argv[1].split(",")) if len(sys.argv) > 1 else _PASSES
    cfg = SimpleNamespace(passes=passes)
    nc, C = build_nc(cfg)
    print("n_instructions:", nc.n_instructions())


# version 11 (final)
# speedup vs baseline: 1.2033x; 1.2033x over previous
import math
from collections import defaultdict
from contextlib import contextmanager
from types import SimpleNamespace

import numpy as np
import ml_dtypes

import concourse.bass as bass
import concourse.mybir as mybir
from concourse.bass_utils import run_bass_kernel_spmd

BF = ml_dtypes.bfloat16
F32 = mybir.dt.float32
BF16 = mybir.dt.bfloat16
AF = mybir.ActivationFunctionType
ALU = mybir.AluOpType
AX = mybir.AxisListType

T, D = 4096, 1024
H4 = 4096
HALF = 512
SCALE = 1.0 / 32.0
NEG = -1.0e30
NCORES = 8

ENG_ATTR = {"sp": "sync", "pe": "tensor", "act": "scalar", "dve": "vector", "pool": "gpsimd"}
ENGS = list(ENG_ATTR)


class Plan:
    def __init__(self, nc):
        self.nc = nc
        self.items = {k: [] for k in ENGS}
        self.cnt = defaultdict(int)
        self.lastwait = {k: defaultdict(int) for k in ENGS}
        self.sem_names = []

    def _sem(self, name):
        if name not in self.cnt:
            self.sem_names.append(name)
            self.cnt[name] = 0

    def op(self, eng, emit, deps=(), inc=None, by=1):
        waits = []
        for d in deps:
            if d is None:
                continue
            s, v = d
            self._sem(s)
            if v > self.lastwait[eng][s]:
                waits.append((s, v))
                self.lastwait[eng][s] = v
        tok = None
        if inc is not None:
            self._sem(inc)
            self.cnt[inc] += by
            tok = (inc, self.cnt[inc])
        self.items[eng].append((waits, emit, inc, by))
        return tok

    def dma(self, q, out_ap, in_ap, sem, deps=(), **kw):
        return self.op(
            q, (lambda e, o=out_ap, i=in_ap, kw=dict(kw): e.dma_start(o, i, **kw)),
            deps, inc=sem, by=16)

    def all_tokens(self):
        return [(s, self.cnt[s]) for s in self.sem_names if self.cnt[s] > 0]

    def barrier(self):
        toks = self.all_tokens()
        for e in ENGS:
            self.op(e, None, deps=toks)

    def emit(self, blk, sems):
        for name, attr in ENG_ATTR.items():
            items = self.items[name]

            def make(items=items):
                def f(eng):
                    for waits, emitfn, inc, by in items:
                        ins = None
                        for (s, v) in waits:
                            ins = eng.wait_ge(sems[s], v)
                        if emitfn is not None:
                            ins = emitfn(eng)
                        if inc is not None:
                            ins.then_inc(sems[inc], by)
                return f

            getattr(blk, attr)(make())


@contextmanager
def scope(nc):
    sb, pb = nc.sbuf_base, nc.psum_base
    yield
    nc.sbuf_base = sb
    nc.psum_base = pb


class TD:
    def __init__(self, C, name):
        self.C, self.name, self.i = C, name, 0
        C.pl._sem(name)

    def start(self):
        if self.i >= 2:
            tok = (self.name, 5 * (self.i - 1))
            for e in ENGS:
                self.C.pl.op(e, None, deps=[tok])

    def end(self):
        for e in ENGS:
            self.C.pl.op(e, lambda en: en.nop(), inc=self.name)
        self.i += 1


def MM(C, out, lhsT, rhs, start, stop, deps=(), inc=None):
    return C.pl.op(
        "pe",
        (lambda e, o=out, l=lhsT, r=rhs, st=start, sp=stop:
         e.matmul(o, l, r, start=st, stop=sp)),
        deps, inc=inc)


def TR(C, out, in_, deps=(), inc=None):
    return C.pl.op(
        "pe", (lambda e, o=out, i=in_: e.transpose(o, i, C.t_ident[:, :])), deps, inc=inc)


def OP(C, eng, emit, deps=(), inc="auto"):
    if inc == "auto":
        inc = {"pe": "Epe", "act": "Eact", "dve": "Edve", "pool": "Epool", "sp": None}[eng]
    return C.pl.op(eng, emit, deps, inc=inc)


def bcast_row(C, row_sb, out_sb, ps, deps):
    MM(C, ps[:, 0:512], C.t_posrow[0:1, :], row_sb[0:1, 0:512], True, True, deps=deps)
    t1 = MM(C, ps[:, 512:1024], C.t_posrow[0:1, :], row_sb[0:1, 512:1024], True, True,
            deps=deps, inc="Epe")
    return OP(C, "act", lambda e, o=out_sb, i=ps: e.copy(o[:, :], i[:, :]), deps=[t1])


def build_nc(cfg):
    nc = bass.Bass("TRN2", target_bir_lowering=False, debug=False)
    pl = Plan(nc)
    C = SimpleNamespace(nc=nc, pl=pl, cfg=cfg)

    def inp(name, shape, dt=F32):
        t = nc.dram_tensor(name, list(shape), dt, kind="ExternalInput")
        setattr(C, "i_" + name, t)
        return t

    inp("x", (2, T, D))
    for n, sh in [("wq", (D, D)), ("wk", (D, D)), ("wv", (D, D)), ("wgate", (D, D)),
                  ("wp", (D, D)), ("wclock", (D, 1)), ("wo", (D, D)),
                  ("w1m", (D, H4)), ("w2m", (H4, D)),
                  ("fw1t", (D, H4)), ("fw2t", (D, H4)), ("fh1r", (32, 128, 32, 128)),
                  ("fh2r", (32, 128, 32, 128)), ("fw3t", (H4, D)),
                  ("gbrow", (1, D)), ("pbrow", (1, D)), ("obrow", (1, D)),
                  ("b2mrow", (1, D)), ("s3row", (1, D)), ("cbrow", (1, 1)),
                  ("g1c", (128, 8)), ("b1c", (128, 8)), ("g2c", (128, 8)), ("b2c", (128, 8)),
                  ("b1m_pc", (128, 32)), ("s1_pc", (128, 32)), ("s2_pc", (128, 32)),
                  ("h1s_pc", (128, 32)), ("h2s_pc", (128, 32)),
                  ("ctri_n", (128, 128)), ("ctri_p", (128, 128)),
                  ("cnegrow", (1, 128)), ("cposrow", (1, 128)), ("conescol", (128, 1)),
                  ("cmasktri", (128, 128)), ("ccos", (T, HALF)), ("csin", (T, HALF)),
                  ("cbias", (128, 3))]:
        inp(n, sh)
    inp("cident", (128, 128), BF16)

    C.out_t = nc.dram_tensor("out", [2, T, D], F32, kind="ExternalOutput")

    def scr(name, shape, dt):
        kind = "ExternalOutput" if name in getattr(cfg, "dev_outputs", ()) else "Internal"
        t = nc.dram_tensor(name, list(shape), dt, kind=kind)
        setattr(C, name, t)
        return t

    for n, sh, dt in [("d_h1R", (32, 128, H4), BF16), ("d_h2R", (32, 128, H4), BF16),
                      ("a_xn", (2, T, D), F32), ("a_xnT", (2, 8, 128, T), BF16),
                      ("a_xn2T", (2, 8, 128, T), BF16), ("a_A", (2, T, D), F32),
                      ("a_ph", (2, T, D), F32), ("a_clock", (2, 128, 32), F32),
                      ("a_pmaxrow", (2, D), F32), ("a_qfT", (2, 8, 128, T), BF16),
                      ("a_kfT", (2, 8, 128, T), BF16), ("a_v", (2, T, D), BF16),
                      ("a_x2", (2, T, D), F32), ("a_hT", (2, 8, 128, T), BF16),
                      ("a_x3", (2, T, D), F32), ("a_h", (2, 4, 128, 32, 1024), BF16),
                      ("a_g1", (2, 4, 128, 32, 1024), BF16),
                      ("a_g2", (2, 4, 128, 32, 1024), BF16)]:
        scr(n, sh, dt)

    C.v_x_q = C.i_x.ap().rearrange("s (q p) d -> s q p d", p=128)
    C.v_x_t = C.i_x.ap().rearrange("s (t u p) d -> s t p u d", p=128, u=4)
    C.v_out_t = C.out_t.ap().rearrange("s (t u p) d -> s t p u d", p=128, u=4)
    C.v_xn_q = C.a_xn.ap().rearrange("s (q p) d -> s q p d", p=128)
    C.v_xn_t = C.a_xn.ap().rearrange("s (t u p) d -> s t p u d", p=128, u=4)
    C.v_A_t = C.a_A.ap().rearrange("s (t u p) d -> s t p u d", p=128, u=4)
    C.v_A_c = C.a_A.ap().rearrange("s (c p) d -> s c p d", p=128)
    C.v_ph_t = C.a_ph.ap().rearrange("s (t u p) d -> s t p u d", p=128, u=4)
    C.v_ph_c = C.a_ph.ap().rearrange("s (c p) d -> s c p d", p=128)
    C.v_xnT = C.a_xnT.ap().rearrange("s c p t -> s p c t")
    C.v_xn2T = C.a_xn2T.ap().rearrange("s c p t -> s p c t")
    C.v_qfT = C.a_qfT.ap().rearrange("s c p t -> s p c t")
    C.v_kfT = C.a_kfT.ap().rearrange("s c p t -> s p c t")
    C.v_v_t = C.a_v.ap().rearrange("s (t u p) d -> s t p u d", p=128, u=4)
    C.v_v_g = C.a_v.ap().rearrange("s (g u p) d -> s g p u d", p=128, u=4)
    C.v_x2_q = C.a_x2.ap().rearrange("s (q p) d -> s q p d", p=128)
    C.v_hT = C.a_hT.ap().rearrange("s c p t -> s p c t")
    C.v_x3_q = C.a_x3.ap().rearrange("s (q p) d -> s q p d", p=128)
    C.v_h = C.a_h.ap()
    C.v_g1 = C.a_g1.ap()
    C.v_g2 = C.a_g2.ap()
    C.v_cos = C.i_ccos.ap().rearrange("(c p) h -> c p h", p=128)
    C.v_sin = C.i_csin.ap().rearrange("(c p) h -> c p h", p=128)

    def sba(name, shape, dt):
        return nc.alloc_sbuf_tensor(name, list(shape), dt)

    C.t_trin = sba("c_trin", (128, 128), F32)
    C.t_trip = sba("c_trip", (128, 128), F32)
    C.t_negrow = sba("c_negrow", (1, 128), F32)
    C.t_posrow = sba("c_posrow", (1, 128), F32)
    C.t_onescol = sba("c_onescol", (128, 1), F32)
    C.t_ident = sba("c_ident", (128, 128), BF16)
    C.t_mask = sba("c_mask", (128, 128), F32)
    C.t_cbias = sba("c_cbias", (128, 3), F32)
    nc.const_aps.register(F32, 1e-5, C.t_cbias[:, 0:1])
    nc.const_aps.register(F32, 1.0, C.t_cbias[:, 1:2])
    nc.const_aps.register(F32, 1e-8, C.t_cbias[:, 2:3])

    tk = None
    for dst, srct in [(C.t_trin, C.i_ctri_n), (C.t_trip, C.i_ctri_p),
                      (C.t_negrow, C.i_cnegrow), (C.t_posrow, C.i_cposrow),
                      (C.t_onescol, C.i_conescol), (C.t_ident, C.i_cident),
                      (C.t_mask, C.i_cmasktri), (C.t_cbias, C.i_cbias)]:
        tk = pl.dma("sp", dst[:, :], srct.ap(), "Dc")
    C.tok_consts = tk

    if "p0" in cfg.passes:
        for srct, dstt in [(C.i_fh1r, C.d_h1R), (C.i_fh2r, C.d_h2R)]:
            for i in range(4):
                sl = slice(i * 8, (i + 1) * 8)
                pl.dma("pool", dstt.ap()[sl],
                       srct.ap()[sl].rearrange("o p c f -> o p (c f)"), "Dp0")

    if "p1" in cfg.passes:
        with scope(nc):
            plan_p12(C)
        pl.barrier()
    if "p3" in cfg.passes:
        with scope(nc):
            plan_p3(C)
        pl.barrier()
    if "p4" in cfg.passes:
        with scope(nc):
            plan_p4(C)
        pl.barrier()
    if "p5" in cfg.passes:
        with scope(nc):
            plan_p5(C)
        pl.barrier()
    if "p6" in cfg.passes:
        with scope(nc):
            plan_p67(C, 1)
        pl.barrier()
        with scope(nc):
            plan_p67(C, 2)
        pl.barrier()
    if "p8" in cfg.passes:
        with scope(nc):
            plan_p8(C)
    pl.barrier()

    sems = {}
    cms = []
    for s in pl.sem_names:
        cm = nc.semaphore(s)
        sems[s] = cm.__enter__()
        cms.append(cm)
    for s in pl.sem_names:
        nc.gpsimd.sem_clear(sems[s])
    nc.all_engine_barrier()
    with nc.Block() as blk:
        pl.emit(blk, sems)
    for s in pl.sem_names:
        nc.gpsimd.sem_clear(sems[s])
    nc.all_engine_barrier()
    for cm in reversed(cms):
        cm.__exit__(None, None, None)
    return nc, C


def plan_p12(C):
    nc, pl = C.nc, C.pl

    def sb(shape, dt):
        C._uid = getattr(C, "_uid", 0) + 1
        return nc.alloc_sbuf_tensor(f"sb{C._uid}", list(shape), dt)

    def ps(shape, dt):
        C._uid = getattr(C, "_uid", 0) + 1
        return nc.alloc_psum_tensor(f"ps{C._uid}", list(shape), dt)

    carry = sb((1, D), F32)
    rowt = sb((1, D), F32)
    C.pmax_b = [sb((128, D), F32), sb((128, D), F32)]

    pj0 = ps((128, D), F32)
    pj1 = ps((128, D), F32)
    sc = ps((128, D), F32)
    tp = [ps((128, D), BF16), ps((128, D), BF16)]

    sc1 = scope(nc); sc1.__enter__()
    w_g = sb((128, 8, D), BF16)
    w_p = sb((128, 8, D), BF16)
    w_c = sb((128, 8, 1), BF16)
    r_row = sb((1, D), F32); r_cb = sb((1, 1), F32)
    t_g1c = sb((128, 8), F32); t_b1c = sb((128, 8), F32)
    gb_b = sb((128, D), F32); pb_b = sb((128, D), F32)
    cb_c = sb((128, 1), F32)

    pl.dma("pool", w_g[:, :, :], C.i_wgate.ap().rearrange("(c p) o -> p c o", p=128), "Dw")
    tok_w = pl.dma("pool", w_p[:, :, :], C.i_wp.ap().rearrange("(c p) o -> p c o", p=128), "Dw")
    tok_wc = pl.dma("pool", w_c[:, :, :], C.i_wclock.ap().rearrange("(c p) o -> p c o", p=128), "Dw",
                    allow_slow_non_contiguous=True)
    tr0 = pl.dma("sp", r_row[0:1, :], C.i_gbrow.ap(), "Dc")
    tr2 = pl.dma("sp", r_cb[0:1, :], C.i_cbrow.ap(), "Dc")
    tr3 = pl.dma("sp", t_g1c[:, :], C.i_g1c.ap(), "Dc")
    tr4 = pl.dma("sp", t_b1c[:, :], C.i_b1c.ap(), "Dc")
    tk_gb = bcast_row(C, r_row, gb_b, pj0, deps=[tr0, C.tok_consts])
    tr1 = pl.dma("sp", r_row[0:1, :], C.i_pbrow.ap(), "Dc", deps=[tk_gb])
    tk_pb = bcast_row(C, r_row, pb_b, pj1, deps=[tr1])
    t_cb1 = MM(C, sc[:, 0:1], C.t_posrow[0:1, :], r_cb[0:1, 0:1], True, True,
               deps=[tr2, C.tok_consts], inc="Epe")
    tk_cb = OP(C, "act", lambda e: e.copy(cb_c[:, :], sc[:, 0:1]), deps=[t_cb1])

    xin = [sb((128, 4, D), F32), sb((128, 4, D), F32)]
    xn_sb = sb((128, 4, D), F32)
    bf1 = sb((128, D), BF16)
    bf2 = sb((128, D), BF16)
    xnT_st = sb((128, 8, 512), BF16)
    xn2T_st = sb((128, 8, 512), BF16)
    g_sb = sb((128, 4, D), F32)
    scr_ = sb((128, 4, D), F32)
    p_sb = sb((128, 4, D), F32)
    A_sb = sb((128, 4, D), F32)
    yb = sb((128, 16), F32)
    clk_all = sb((128, 32), F32)
    M_sb = sb((128, D), F32)
    pmaxT = sb((128, 8), F32)
    prow = sb((1, D), F32)
    stt = sb((128, 32, 1), F32)

    td = TD(C, "T1")
    xn_store = None; xnT_store = None; xn2T_store = None
    A_store = None; p_store = None; clk_store = None
    pj0_free = tk_gb; pj1_free = tk_pb; sc_free = tk_cb
    carry_tok = None
    M_free = None
    bf1_free = None; bf2_free = None
    tp_last = [None, None]
    scr_free = None
    g_free = None
    sp_free = None
    xn_free = None
    tA_prev = None
    clk_tok = None
    pmaxT_free = None

    for s in range(2):
        t_m0 = OP(C, "pool", lambda e: e.memset(M_sb[:, :], -3.0e38), deps=[M_free])
        carry_tok = OP(C, "dve", lambda e: e.memset(carry[0:1, :], 0.0), deps=[carry_tok])
        M_tok = t_m0

        for tt in range(8):
            td.start()
            b = tt % 2
            gi = s * 8 + tt
            pb2 = 16 * (gi % 2)
            s1 = stt[:, pb2 + 0:pb2 + 4, :]; ssq = stt[:, pb2 + 4:pb2 + 8, :]
            mcol = stt[:, pb2 + 8:pb2 + 12, :]; rstd = stt[:, pb2 + 12:pb2 + 16, :]
            yo = 8 * (gi % 2)
            ybuf = yb[:, yo:yo + 4]; ysp = yb[:, yo + 4:yo + 8]
            tload = pl.dma("sp", xin[b][:, :, :], C.v_x_t[s, tt], f"Dx{b}",
                           deps=[tk_gb, tk_pb, tk_cb] if gi == 0 else ())

            def ln_stats(src_ap, dep):
                t_s1 = OP(C, "dve", lambda e: e.reduce_sum(s1[:, :, :], src_ap, axis=AX.X),
                          deps=[dep])
                t_sq = OP(C, "act", lambda e: e.activation(scr_[:, :, :], src_ap, AF.Square),
                          deps=[dep, scr_free])
                t_ssq = OP(C, "dve", lambda e: e.reduce_sum(ssq[:, :, :], scr_[:, :, :], axis=AX.X),
                           deps=[t_sq])
                t_m = OP(C, "dve", lambda e: e.tensor_scalar_mul(mcol[:, :, :], s1[:, :, :], 1.0 / D),
                         deps=[t_s1])
                t_v = OP(C, "dve", lambda e: e.tensor_scalar(
                    ssq[:, :, :], ssq[:, :, :], 1.0 / D, None, ALU.mult), deps=[t_ssq])
                t_mm = OP(C, "dve", lambda e: e.tensor_mul(s1[:, :, :], mcol[:, :, :], mcol[:, :, :]),
                          deps=[t_m])
                t_v = OP(C, "dve", lambda e: e.tensor_sub(ssq[:, :, :], ssq[:, :, :], s1[:, :, :]),
                         deps=[t_v, t_mm])
                t_l = OP(C, "act", lambda e: e.activation(s1[:, :, :], ssq[:, :, :], AF.Ln, bias=1e-5),
                         deps=[t_v, C.tok_consts])
                t_r = OP(C, "act", lambda e: e.activation(rstd[:, :, :], s1[:, :, :], AF.Exp, scale=-0.5),
                         deps=[t_l])
                return t_r, t_ssq

            t_r1, scr_free = ln_stats(xin[b][:, :, :], tload)
            t_xn = []
            for u in range(4):
                t_xn.append(OP(
                    C, "dve",
                    lambda e, b=b, u=u: e.tensor_scalar(
                        xn_sb[:, u, :], xin[b][:, u, :], mcol[:, u, :], rstd[:, u, :],
                        ALU.subtract, ALU.mult),
                    deps=[t_r1, xn_store, xn_free]))
            xn_store = pl.dma("sp", C.v_xn_t[s, tt], xn_sb[:, :, :], "Sxn", deps=[t_xn[-1]])

            t_a = OP(C, "dve", lambda e: e.tensor_mul(
                s1[:, :, :], rstd[:, :, :], rstd[:, :, :]), deps=[t_r1])
            t_b = OP(C, "dve", lambda e: e.tensor_mul(
                ssq[:, :, :], ssq[:, :, :], s1[:, :, :]), deps=[t_a, scr_free])
            t_c = OP(C, "act", lambda e: e.activation(
                s1[:, :, :], ssq[:, :, :], AF.Ln, bias=1e-5), deps=[t_b, C.tok_consts])
            t_r2 = OP(C, "act", lambda e: e.activation(
                mcol[:, :, :], s1[:, :, :], AF.Exp, scale=-0.5),
                deps=[t_c, t_xn[-1]])
            t_xnn = []
            for u in range(4):
                t_xnn.append(OP(
                    C, "dve",
                    lambda e, b=b, u=u: e.tensor_scalar_mul(
                        xin[b][:, u, :], xn_sb[:, u, :], mcol[:, u, :]),
                    deps=[t_r2]))

            tpA = tp[gi % 2]
            x2st = xn2T_st
            ev1_toks = []; ev2_toks = []
            tp_free = tp_last[gi % 2]
            for u in range(4):
                t_c1 = OP(C, "act", lambda e, u=u: e.copy(bf1[:, :], xn_sb[:, u, :]),
                          deps=[t_xn[u], bf1_free])
                tr_ = None
                for ic in range(8):
                    tr_ = TR(C, tpA[:, ic * 128:(ic + 1) * 128],
                             bf1[:, ic * 128:(ic + 1) * 128],
                             deps=[t_c1, tp_free, C.tok_consts] if ic == 0 else (),
                             inc="Epe" if ic == 7 else None)
                bf1_free = tr_
                ev1 = OP(C, "dve",
                         lambda e, u=u, tpA=tpA: e.tensor_copy(
                             xnT_st[:, :, u * 128:(u + 1) * 128],
                             tpA[:, :].rearrange("p (c f) -> p c f", f=128)),
                         deps=[tr_, xnT_store])
                ev1_toks.append(ev1)

                t_c2 = OP(C, "act", lambda e, b=b, u=u: e.copy(bf2[:, :], xin[b][:, u, :]),
                          deps=[t_xnn[u], bf2_free])
                tr2_ = None
                for ic in range(8):
                    tr2_ = TR(C, tpA[:, ic * 128:(ic + 1) * 128],
                              bf2[:, ic * 128:(ic + 1) * 128],
                              deps=[t_c2, ev1] if ic == 0 else (),
                              inc="Epe" if ic == 7 else None)
                bf2_free = tr2_
                last = None
                for ic in range(8):
                    last = OP(C, "dve",
                              lambda e, u=u, ic=ic, x2st=x2st, tpA=tpA: e.tensor_scalar(
                                  x2st[:, ic, u * 128:(u + 1) * 128],
                                  tpA[:, ic * 128:(ic + 1) * 128],
                                  t_g1c[:, ic:ic + 1], t_b1c[:, ic:ic + 1],
                                  ALU.mult, ALU.add),
                              deps=[tr2_, xn2T_store, tr3, tr4])
                tp_free = last
                ev2_toks.append(last)

            tp_last[gi % 2] = tp_free
            xnT_store = pl.dma("sp", C.v_xnT[s, :, :, tt * 512:(tt + 1) * 512],
                               xnT_st[:, :, :], "SxnT", deps=ev1_toks)
            xn2T_store = pl.dma(
                "sp", C.v_xn2T[s, :, :, tt * 512:(tt + 1) * 512],
                x2st[:, :, :], "Sx2T", deps=ev2_toks)

            gev = []; pev = []; yev = []
            for u in range(4):
                tg = None
                for half in range(2):
                    for ic in range(8):
                        tg = MM(C, pj0[:, half * 512:(half + 1) * 512],
                                x2st[:, ic, u * 128:(u + 1) * 128],
                                w_g[:, ic, half * 512:(half + 1) * 512],
                                start=(ic == 0), stop=(ic == 7),
                                deps=[ev2_toks[u], tok_w, pj0_free] if (ic == 0 and half == 0) else (),
                                inc="Epe" if (ic == 7 and half == 1) else None)
                tge = OP(C, "dve",
                         lambda e, u=u: e.scalar_tensor_tensor(
                             g_sb[:, u, :], gb_b[:, :], 1.0, pj0[:, :], ALU.bypass, ALU.add),
                         deps=[tg, tk_gb, g_free])
                pj0_free = tge
                gev.append(tge)

                tp_ = None
                for half in range(2):
                    for ic in range(8):
                        tp_ = MM(C, pj1[:, half * 512:(half + 1) * 512],
                                 x2st[:, ic, u * 128:(u + 1) * 128],
                                 w_p[:, ic, half * 512:(half + 1) * 512],
                                 start=(ic == 0), stop=(ic == 7),
                                 deps=[ev2_toks[u], tok_w, pj1_free] if (ic == 0 and half == 0) else (),
                                 inc="Epe" if (ic == 7 and half == 1) else None)
                tpe = OP(C, "dve",
                         lambda e, u=u: e.scalar_tensor_tensor(
                             p_sb[:, u, :], pb_b[:, :], 1.0, pj1[:, :], ALU.bypass, ALU.add),
                         deps=[tp_, tk_pb, p_store, M_tok])
                pj1_free = tpe
                pev.append(tpe)
                M_tok = OP(C, "dve",
                           lambda e, u=u: e.tensor_max(M_sb[:, :], p_sb[:, u, :], M_sb[:, :]),
                           deps=[tpe, M_tok])

                tc = None
                for ic in range(8):
                    tc = MM(C, pj0[:, 0:1], x2st[:, ic, u * 128:(u + 1) * 128],
                            w_c[:, ic, :], start=(ic == 0), stop=(ic == 7),
                            deps=[tge, tok_wc] if ic == 0 else (),
                            inc="Epe" if ic == 7 else None)
                ty = OP(C, "dve", lambda e, u=u, ybuf=ybuf: e.tensor_add(
                    ybuf[:, u:u + 1], pj0[:, 0:1], cb_c[:, :]), deps=[tc, tk_cb, clk_tok])
                pj0_free = ty
                yev.append(ty)

            ta0 = OP(C, "act", lambda e, ysp=ysp: e.activation(ysp[:, :], ybuf[:, :], AF.Abs),
                     deps=yev + [clk_tok])
            ta = OP(C, "act", lambda e, ysp=ysp: e.activation(ysp[:, :], ysp[:, :], AF.Exp, scale=-1.0),
                    deps=[ta0])
            ta = OP(C, "act", lambda e, ysp=ysp: e.activation(ysp[:, :], ysp[:, :], AF.Ln, bias=1.0),
                    deps=[ta, C.tok_consts])
            trl = OP(C, "dve", lambda e, ybuf=ybuf: e.tensor_scalar_max(ybuf[:, :], ybuf[:, :], 0.0),
                     deps=yev + [ta0])
            clk_tok = OP(C, "pool", lambda e, tt=tt, ybuf=ybuf, ysp=ysp: e.tensor_add(
                clk_all[:, tt * 4:(tt + 1) * 4], ybuf[:, :], ysp[:, :]),
                deps=[ta, trl, clk_store])

            ta0 = OP(C, "act", lambda e: e.activation(scr_[:, :, :], g_sb[:, :, :], AF.Abs),
                     deps=gev + [scr_free])
            ta = OP(C, "act", lambda e: e.activation(scr_[:, :, :], scr_[:, :, :], AF.Exp, scale=-1.0),
                    deps=[ta0])
            ta = OP(C, "act", lambda e: e.activation(scr_[:, :, :], scr_[:, :, :], AF.Ln, bias=1.0),
                    deps=[ta, C.tok_consts])
            trl = None
            for u in range(4):
                trl = OP(C, "pool", lambda e, u=u: e.tensor_scalar_max(
                    A_sb[:, u, :], g_sb[:, u, :], 0.0),
                    deps=(gev + [ta0, A_store, tA_prev]) if u == 0 else ())
            tsp = OP(C, "pool", lambda e: e.tensor_add(g_sb[:, :, :], A_sb[:, :, :], scr_[:, :, :]),
                     deps=[ta, trl, sp_free])
            scr_free = tsp

            clip_toks = []
            for u in range(4):
                tmm = None
                for half in range(2):
                    tmm = MM(C, sc[:, half * 512:(half + 1) * 512], C.t_trin[:, :],
                             g_sb[:, u, half * 512:(half + 1) * 512], True, False,
                             deps=[tsp, sc_free, carry_tok, C.tok_consts] if half == 0 else ())
                for half in range(2):
                    tmm = MM(C, sc[:, half * 512:(half + 1) * 512], C.t_negrow[0:1, :],
                             carry[0:1, half * 512:(half + 1) * 512], False, True,
                             deps=[carry_tok], inc="Epe" if half == 1 else None)
                tclip = OP(C, "dve", lambda e, u=u: e.tensor_scalar(
                    A_sb[:, u, :], sc[:, :], 50.0, -60.0, ALU.min, ALU.max),
                    deps=[tmm, A_store, tA_prev, tsp])
                clip_toks.append(tclip)
                tcs = None
                for half in range(2):
                    tcs = MM(C, sc[0:1, half * 512:(half + 1) * 512], C.t_onescol[:, :],
                             g_sb[:, u, half * 512:(half + 1) * 512], True, True,
                             deps=[tclip] if half == 0 else (),
                             inc="Epe" if half == 1 else None)
                trw = OP(C, "act", lambda e: e.copy(rowt[0:1, :], sc[0:1, :]),
                         deps=[tcs, carry_tok])
                carry_tok = OP(C, "dve", lambda e: e.tensor_add(
                    carry[0:1, :], carry[0:1, :], rowt[0:1, :]), deps=[trw, carry_tok])
                sc_free = trw
                sp_free = tcs
                g_free = tcs

            tA_prev = OP(C, "act", lambda e: e.activation(A_sb[:, :, :], A_sb[:, :, :], AF.Exp),
                         deps=clip_toks)
            A_store = pl.dma("sp", C.v_A_t[s, tt], A_sb[:, :, :], "SA", deps=[tA_prev])
            p_store = pl.dma("sp", C.v_ph_t[s, tt], p_sb[:, :, :], "Sp", deps=pev + [M_tok])
            td.end()

        td.start()
        clk_store = pl.dma("sp", C.a_clock.ap()[s], clk_all[:, :], "Sclk", deps=[clk_tok])
        tpA = tp[0]
        tMc = OP(C, "act", lambda e: e.copy(bf1[:, :], M_sb[:, :]), deps=[M_tok, bf1_free])
        trM = None
        for ic in range(8):
            trM = TR(C, tpA[:, ic * 128:(ic + 1) * 128], bf1[:, ic * 128:(ic + 1) * 128],
                     deps=[tMc, tp_last[0]] if ic == 0 else (), inc="Epe" if ic == 7 else None)
        M_free = tMc
        bf1_free = trM
        red = []
        for ic in range(8):
            red.append(OP(C, "dve", lambda e, ic=ic, tpA=tpA: e.reduce_max(
                pmaxT[:, ic:ic + 1], tpA[:, ic * 128:(ic + 1) * 128], axis=AX.X),
                deps=[trM, pmaxT_free]))
        tst = pl.dma("sp", C.a_pmaxrow.ap()[s, :].rearrange("(c p) -> p c", p=128),
                     pmaxT[:, :], "Spm", deps=red, allow_slow_non_contiguous=True)
        pmaxT_free = tst
        tp_last[0] = red[-1]
        tld = pl.dma("sp", prow[0:1, :], C.a_pmaxrow.ap()[s:s + 1, :], "Dc", deps=[tst])
        MM(C, sc[:, 0:512], C.t_posrow[0:1, :], prow[0:1, 0:512], True, True,
           deps=[tld, sc_free])
        tb1 = MM(C, sc[:, 512:1024], C.t_posrow[0:1, :], prow[0:1, 512:1024], True, True,
                 deps=[tld], inc="Epe")
        tpm = OP(C, "act", lambda e, s=s: e.copy(C.pmax_b[s][:, :], sc[:, :]), deps=[tb1])
        sc_free = tpm
        td.end()

    pl.barrier()
    sc1.__exit__(None, None, None)

    sc2 = scope(nc); sc2.__enter__()
    w_q = sb((128, 8, D), BF16)
    w_k = sb((128, 8, D), BF16)
    w_v = sb((128, 8, D), BF16)
    tok_w2 = None
    for dst, src in [(w_q, C.i_wq), (w_k, C.i_wk), (w_v, C.i_wv)]:
        tok_w2 = pl.dma("pool", dst[:, :, :],
                        src.ap().rearrange("(c p) o -> p c o", p=128), "Dw")

    xn2T_in = [sb((128, 8, 512), BF16), sb((128, 8, 512), BF16)]
    A_in = [sb((128, D), F32), sb((128, D), F32)]
    ph_in = [sb((128, D), F32), sb((128, D), F32)]
    cs_in = [sb((128, 2, HALF), F32), sb((128, 2, HALF), F32)]
    clock_sb = sb((128, 32), F32)
    qa = sb((128, D), F32); ka = sb((128, D), F32)
    qr = sb((128, D), F32); kr = sb((128, D), F32)
    tq1 = sb((128, HALF), F32); tq2 = sb((128, HALF), F32)
    tk1 = sb((128, HALF), F32); tk2b = sb((128, HALF), F32)
    pex = sb((128, D), F32)
    pe_sb = sb((128, D), F32)
    pc_sb = sb((128, D), F32)
    rc_sb = sb((128, D), F32)
    t2a = sb((128, D), F32); t2r = sb((128, D), F32)
    qf1 = sb((128, D), F32); kk = sb((128, D), F32)
    qf_bf = sb((128, D), BF16); kf_bf = sb((128, D), BF16)
    qfT_st = sb((128, 8, 512), BF16); kfT_st = sb((128, 8, 512), BF16)
    v_st = sb((128, 4, D), BF16)
    st3 = sb((128, 12, 1), F32)

    td2 = TD(C, "T2")
    qfT_store = None; kfT_store = None; v_store = None
    pj0_free = None; pj1_free = None; sc_free2 = sc_free
    x2Tw = [None, None]
    qa_free = None; ka_free = None
    pe_free_pe = None; pe_free_pl = None
    pex_free = None; pc_free = None; t2r_free = None
    qf_free = None; kf_free = None
    clk_ld = None
    carry_tok2 = carry_tok

    def r2(ap3):
        return ap3.rearrange("p (h one) -> p h one", one=1)

    for s in range(2):
        clk_ld = pl.dma("sp", clock_sb[:, :], C.a_clock.ap()[s], "Dc", deps=[clk_store])
        carry_tok2 = OP(C, "dve", lambda e: e.memset(carry[0:1, :], 0.0), deps=[carry_tok2])
        for cc in range(32):
            td2.start()
            tt, u = cc // 4, cc % 4
            gj = s * 32 + cc
            cb = gj % 2
            b6 = 6 * cb
            if u == 0:
                x2Tw[tt % 2] = pl.dma("sp", xn2T_in[tt % 2][:, :, :],
                                      C.v_xn2T[s, :, :, tt * 512:(tt + 1) * 512], f"Dq{tt % 2}")
            xT = xn2T_in[tt % 2]
            tA_ = pl.dma("pool", A_in[cb][:, :], C.v_A_c[s, cc], f"Da{cb}")
            tP_ = pl.dma("pool", ph_in[cb][:, :], C.v_ph_c[s, cc], f"Dph{cb}")
            pl.dma("sp", cs_in[cb][:, 0, :], C.v_cos[cc], f"Dcs{cb}")
            tCS = pl.dma("sp", cs_in[cb][:, 1, :], C.v_sin[cc], f"Dcs{cb}")

            tmq = None
            for half in range(2):
                for ic in range(8):
                    tmq = MM(C, pj0[:, half * 512:(half + 1) * 512],
                             xT[:, ic, u * 128:(u + 1) * 128],
                             w_q[:, ic, half * 512:(half + 1) * 512],
                             start=(ic == 0), stop=(ic == 7),
                             deps=[x2Tw[tt % 2], tok_w2, pj0_free] if (ic == 0 and half == 0) else (),
                             inc="Epe" if (ic == 7 and half == 1) else None)
            teq = OP(C, "act", lambda e: e.copy(qa[:, :], pj0[:, :]), deps=[tmq, qa_free])
            pj0_free = teq
            tmk = None
            for half in range(2):
                for ic in range(8):
                    tmk = MM(C, pj1[:, half * 512:(half + 1) * 512],
                             xT[:, ic, u * 128:(u + 1) * 128],
                             w_k[:, ic, half * 512:(half + 1) * 512],
                             start=(ic == 0), stop=(ic == 7),
                             deps=[x2Tw[tt % 2], tok_w2, pj1_free] if (ic == 0 and half == 0) else (),
                             inc="Epe" if (ic == 7 and half == 1) else None)
            tek = OP(C, "act", lambda e: e.copy(ka[:, :], pj1[:, :]), deps=[tmk, ka_free])
            pj1_free = tek
            tmv = None
            for half in range(2):
                for ic in range(8):
                    tmv = MM(C, pj0[:, half * 512:(half + 1) * 512],
                             xT[:, ic, u * 128:(u + 1) * 128],
                             w_v[:, ic, half * 512:(half + 1) * 512],
                             start=(ic == 0), stop=(ic == 7),
                             deps=[teq] if (ic == 0 and half == 0) else (),
                             inc="Epe" if (ic == 7 and half == 1) else None)
            tev = OP(C, "act", lambda e, u=u: e.copy(v_st[:, u, :], pj0[:, :]),
                     deps=[tmv, v_store])
            pj0_free = tev

            tl2q = OP(C, "dve", lambda e, b6=b6: e.scalar_tensor_tensor(
                pex[:, :], qa[:, :], 1.0, qa[:, :], ALU.bypass, ALU.mult,
                accum_out=st3[:, b6 + 0:b6 + 1, 0]), deps=[teq, pex_free])
            tq_ = OP(C, "act", lambda e, b6=b6: e.activation(
                st3[:, b6 + 2:b6 + 3, :], st3[:, b6 + 0:b6 + 1, :], AF.Ln), deps=[tl2q])
            tq_ = OP(C, "act", lambda e, b6=b6: e.activation(
                st3[:, b6 + 2:b6 + 3, :], st3[:, b6 + 2:b6 + 3, :], AF.Exp, scale=-0.5), deps=[tq_])
            tqmin = OP(C, "dve", lambda e, b6=b6: e.tensor_scalar_min(
                st3[:, b6 + 4:b6 + 5, :], st3[:, b6 + 2:b6 + 3, :], 1.0e12), deps=[tq_])

            tl2k = OP(C, "dve", lambda e, b6=b6: e.scalar_tensor_tensor(
                kk[:, :], ka[:, :], 1.0, ka[:, :], ALU.bypass, ALU.mult,
                accum_out=st3[:, b6 + 1:b6 + 2, 0]), deps=[tek, kf_free])
            tk_ = OP(C, "act", lambda e, b6=b6: e.activation(
                st3[:, b6 + 3:b6 + 4, :], st3[:, b6 + 1:b6 + 2, :], AF.Ln), deps=[tl2k])
            tk_ = OP(C, "act", lambda e, b6=b6: e.activation(
                st3[:, b6 + 3:b6 + 4, :], st3[:, b6 + 3:b6 + 4, :], AF.Exp, scale=-0.5), deps=[tk_])
            tkmin = OP(C, "dve", lambda e, b6=b6: e.tensor_scalar_min(
                st3[:, b6 + 5:b6 + 6, :], st3[:, b6 + 3:b6 + 4, :], 1.0e12), deps=[tk_])

            CC = cs_in[cb][:, 0, :]
            SS = cs_in[cb][:, 1, :]
            t1 = OP(C, "dve", lambda e, CC=CC: e.tensor_mul(
                tq1[:, :], qa[:, 0:HALF], CC), deps=[teq, tCS])
            t2 = OP(C, "dve", lambda e, SS=SS: e.tensor_mul(
                tq2[:, :], qa[:, HALF:D], SS), deps=[teq, tCS])
            tqe = OP(C, "dve", lambda e: e.tensor_sub(
                qr[:, 0:HALF], tq1[:, :], tq2[:, :]), deps=[t1, t2])
            t1 = OP(C, "dve", lambda e, SS=SS: e.tensor_mul(
                tq1[:, :], qa[:, 0:HALF], SS), deps=[tqe])
            t2 = OP(C, "dve", lambda e, CC=CC: e.tensor_mul(
                tq2[:, :], qa[:, HALF:D], CC), deps=[tqe])
            tqo = OP(C, "dve", lambda e: e.tensor_add(
                qr[:, HALF:D], tq1[:, :], tq2[:, :]), deps=[t1, t2])
            qa_free = tqo

            k1 = OP(C, "pool", lambda e, CC=CC: e.tensor_mul(
                tk1[:, :], ka[:, 0:HALF], CC), deps=[tek, tCS])
            k2 = OP(C, "pool", lambda e, SS=SS: e.tensor_mul(
                tk2b[:, :], ka[:, HALF:D], SS), deps=[tek, tCS])
            tke = OP(C, "pool", lambda e: e.tensor_sub(
                kr[:, 0:HALF], tk1[:, :], tk2b[:, :]), deps=[k1, k2])
            k1 = OP(C, "pool", lambda e, SS=SS: e.tensor_mul(
                tk1[:, :], ka[:, 0:HALF], SS), deps=[tke])
            k2 = OP(C, "pool", lambda e, CC=CC: e.tensor_mul(
                tk2b[:, :], ka[:, HALF:D], CC), deps=[tke])
            tko = OP(C, "pool", lambda e: e.tensor_add(
                kr[:, HALF:D], tk1[:, :], tk2b[:, :]), deps=[k1, k2])
            ka_free = tko

            ts_ = OP(C, "dve", lambda e, cb=cb, s=s: e.tensor_sub(
                pex[:, :], ph_in[cb][:, :], C.pmax_b[s][:, :]), deps=[tP_, tl2q])
            pex_free = te_ = OP(C, "act", lambda e: e.activation(pe_sb[:, :], pex[:, :], AF.Exp),
                                deps=[ts_, pe_free_pe, pe_free_pl])
            tpe_ = OP(C, "dve", lambda e, cc=cc: e.tensor_scalar_mul(
                pe_sb[:, :], pe_sb[:, :], clock_sb[:, cc:cc + 1]), deps=[te_, clk_ld])

            tmm = None
            for half in range(2):
                tmm = MM(C, sc[:, half * 512:(half + 1) * 512], C.t_trip[:, :],
                         pe_sb[:, half * 512:(half + 1) * 512], True, False,
                         deps=[tpe_, sc_free2, carry_tok2] if half == 0 else ())
            for half in range(2):
                tmm = MM(C, sc[:, half * 512:(half + 1) * 512], C.t_posrow[0:1, :],
                         carry[0:1, half * 512:(half + 1) * 512], False, True,
                         deps=[carry_tok2], inc="Epe" if half == 1 else None)
            tpc = OP(C, "act", lambda e: e.activation(pc_sb[:, :], sc[:, :], AF.Ln,
                                                      bias=1e-8), deps=[tmm, pc_free, C.tok_consts])
            trc = OP(C, "act", lambda e: e.activation(rc_sb[:, :], pc_sb[:, :], AF.Exp,
                                                      scale=-1.0), deps=[tpc])
            pc_free = trc
            tcs = None
            for half in range(2):
                tcs = MM(C, sc[0:1, half * 512:(half + 1) * 512], C.t_onescol[:, :],
                         pe_sb[:, half * 512:(half + 1) * 512], True, True,
                         deps=[tpc] if half == 0 else (),
                         inc="Epe" if half == 1 else None)
            trw = OP(C, "act", lambda e: e.copy(rowt[0:1, :], sc[0:1, :]),
                     deps=[tcs, carry_tok2])
            carry_tok2 = OP(C, "dve", lambda e: e.tensor_add(
                carry[0:1, :], carry[0:1, :], rowt[0:1, :]), deps=[trw, carry_tok2])
            sc_free2 = trw
            pe_free_pe = tcs

            tq_ = OP(C, "dve", lambda e, cb=cb: e.scalar_tensor_tensor(
                qf1[:, :], rc_sb[:, :], SCALE, A_in[cb][:, :], ALU.mult, ALU.mult),
                deps=[trc, tA_])
            tqf = OP(C, "dve", lambda e, b6=b6: e.scalar_tensor_tensor(
                qf_bf[:, :], qr[:, :], st3[:, b6 + 4:b6 + 5, 0], qf1[:, :],
                ALU.mult, ALU.mult),
                deps=[tq_, tqmin, tqe, tqo, qf_free])
            ta_ = OP(C, "act", lambda e, cb=cb: e.activation(
                t2a[:, :], A_in[cb][:, :], AF.Ln, bias=1.0e-8),
                deps=[tA_, t2r_free, C.tok_consts])
            ta_ = OP(C, "act", lambda e: e.activation(
                t2r[:, :], t2a[:, :], AF.Exp, scale=-1.0), deps=[ta_])
            tkk = OP(C, "pool", lambda e: e.tensor_mul(kk[:, :], kr[:, :], pe_sb[:, :]),
                     deps=[tko, tpe_, tl2k])
            pe_free_pl = tkk
            tkf = OP(C, "dve", lambda e, b6=b6: e.scalar_tensor_tensor(
                kf_bf[:, :], kk[:, :], st3[:, b6 + 5:b6 + 6, 0], t2r[:, :],
                ALU.mult, ALU.mult),
                deps=[tkk, tkmin, ta_, kf_free])
            t2r_free = tkf

            tpA = tp[gj % 2]
            trq = None
            for ic in range(8):
                trq = TR(C, tpA[:, ic * 128:(ic + 1) * 128],
                         qf_bf[:, ic * 128:(ic + 1) * 128],
                         deps=[tqf, tp_last[gj % 2]] if ic == 0 else (),
                         inc="Epe" if ic == 7 else None)
            qf_free = trq
            evq = OP(C, "dve", lambda e, u=u, tpA=tpA: e.tensor_copy(
                qfT_st[:, :, u * 128:(u + 1) * 128],
                tpA[:, :].rearrange("p (c f) -> p c f", f=128)),
                deps=[trq, qfT_store])
            trk = None
            for ic in range(8):
                trk = TR(C, tpA[:, ic * 128:(ic + 1) * 128],
                         kf_bf[:, ic * 128:(ic + 1) * 128],
                         deps=[tkf, evq] if ic == 0 else (), inc="Epe" if ic == 7 else None)
            kf_free = trk
            evk = OP(C, "dve", lambda e, u=u, tpA=tpA: e.tensor_copy(
                kfT_st[:, :, u * 128:(u + 1) * 128],
                tpA[:, :].rearrange("p (c f) -> p c f", f=128)),
                deps=[trk, kfT_store])
            tp_last[gj % 2] = evk

            if u == 3:
                qfT_store = pl.dma("sp", C.v_qfT[s, :, :, tt * 512:(tt + 1) * 512],
                                   qfT_st[:, :, :], "Sqf", deps=[evq])
                kfT_store = pl.dma("sp", C.v_kfT[s, :, :, tt * 512:(tt + 1) * 512],
                                   kfT_st[:, :, :], "Skf", deps=[evk])
                v_store = pl.dma("sp", C.v_v_t[s, tt], v_st[:, :, :], "Sv", deps=[tev])
            td2.end()
    sc2.__exit__(None, None, None)


def plan_p3(C):
    nc, pl = C.nc, C.pl

    def sb(shape, dt):
        C._uid = getattr(C, "_uid", 0) + 1
        return nc.alloc_sbuf_tensor(f"sb{C._uid}", list(shape), dt)

    def ps(shape, dt):
        C._uid = getattr(C, "_uid", 0) + 1
        return nc.alloc_psum_tensor(f"ps{C._uid}", list(shape), dt)

    w_o = sb((128, 8, D), BF16)
    tok_wo = pl.dma("pool", w_o[:, :, :],
                    C.i_wo.ap().rearrange("(c p) o -> p c o", p=128), "Dw")
    r_ob = sb((1, D), F32)
    t_g2c = sb((128, 8), F32); t_b2c = sb((128, 8), F32)
    ob_b = sb((128, D), F32)
    tro = pl.dma("sp", r_ob[0:1, :], C.i_obrow.ap(), "Dc")
    trg = pl.dma("sp", t_g2c[:, :], C.i_g2c.ap(), "Dc")
    trb = pl.dma("sp", t_b2c[:, :], C.i_b2c.ap(), "Dc")

    s_ps = [ps((128, 512), F32), ps((128, 512), F32)]
    pt_ps = [ps((128, 512), BF16), ps((128, 512), BF16)]
    o_ps = [ps((128, D), F32), ps((128, D), F32)]

    tk_ob = bcast_row(C, r_ob, ob_b, o_ps[0], deps=[tro, C.tok_consts])

    qfT_blk = [[sb((128, 8, 512), BF16) for _ in range(2)] for _ in range(2)]
    kfT_g = [[sb((128, 8, 512), BF16) for _ in range(2)] for _ in range(2)]
    v_g = [[sb((128, 4, D), BF16) for _ in range(2)] for _ in range(2)]
    O = [[sb((128, D), F32) for _ in range(4)] for _ in range(2)]
    P_sb = [sb((128, 512), F32), sb((128, 512), F32)]
    Pb = [sb((128, 512), BF16), sb((128, 512), BF16)]
    ptb = [sb((128, 512), BF16), sb((128, 512), BF16)]
    nm = [sb((128, 4, 1), F32) for _ in range(2)]
    lsum = [sb((128, 4, 1), F32) for _ in range(2)]
    sml = sb((128, 16, 1), F32)
    xn_in = [sb((128, D), F32) for _ in range(2)]
    xnin_free = [None, None]
    at_bf = [sb((128, D), BF16), sb((128, D), BF16)]
    atT = [sb((128, 8, 128), BF16), sb((128, 8, 128), BF16)]
    x2a = [sb((128, D), F32), sb((128, D), F32)]
    x2b = [sb((128, D), F32), sb((128, D), F32)]
    hnb = [sb((128, D), BF16), sb((128, D), BF16)]
    hT_st = [sb((128, 8, 128), BF16), sb((128, 8, 128), BF16)]
    st4 = sb((128, 16, 1), F32)

    td = TD(C, "T3")
    O_tok = [[tk_ob] * 4 for _ in range(2)]
    nm_tok = [[None] * 4 for _ in range(2)]
    l_tok = [[None] * 4 for _ in range(2)]
    x2_store = [None, None]
    hT_store = [None, None]
    kvw = [[None, None], [None, None]]
    qw = [[None, None], [None, None]]
    first_pv_guard = tk_ob

    def emit_A(s, qb, jg, qt):
        if jg == 0 and qt == 0:
            qw[s][qb % 2] = pl.dma("sp" if s == 0 else "pool",
                                   qfT_blk[s][qb % 2][:, :, :],
                                   C.v_qfT[s, :, :, qb * 512:(qb + 1) * 512],
                                   f"Dq{s}{qb % 2}")
        if qt == 0:
            kvw[s][jg % 2] = pl.dma("sp" if s == 0 else "pool",
                                    kfT_g[s][jg % 2][:, :, :],
                                    C.v_kfT[s, :, :, jg * 512:(jg + 1) * 512],
                                    f"Dkv{s}{jg % 2}")
            kvw[s][jg % 2] = pl.dma("sp" if s == 0 else "pool",
                                    v_g[s][jg % 2][:, :, :], C.v_v_g[s, jg],
                                    f"Dkv{s}{jg % 2}")
        r = s
        sp_ = s_ps[r]
        tm = None
        for ic in range(8):
            tm = MM(C, sp_[:, :], qfT_blk[s][qb % 2][:, ic, qt * 128:(qt + 1) * 128],
                    kfT_g[s][jg % 2][:, ic, :], start=(ic == 0), stop=(ic == 7),
                    deps=[qw[s][qb % 2], kvw[s][jg % 2]] if ic == 0 else (),
                    inc="Epe" if ic == 7 else None)
        if jg == qb:
            for j2 in range(4):
                kt = jg * 4 + j2
                qtg = qb * 4 + qt
                if kt == qtg:
                    tm = OP(C, "dve", lambda e, sp_=sp_, j2=j2: e.tensor_add(
                        sp_[:, j2 * 128:(j2 + 1) * 128],
                        sp_[:, j2 * 128:(j2 + 1) * 128], C.t_mask[:, :]),
                        deps=[tm, C.tok_consts])
                elif kt > qtg:
                    tm = OP(C, "dve", lambda e, sp_=sp_, j2=j2: e.tensor_scalar_add(
                        sp_[:, j2 * 128:(j2 + 1) * 128],
                        sp_[:, j2 * 128:(j2 + 1) * 128], NEG), deps=[tm])
        tmx = OP(C, "dve", lambda e, sp_=sp_, r=r: e.reduce_max(
            sml[:, r:r + 1, :], sp_[:, :], axis=AX.X, negate=True), deps=[tm])
        if jg == 0:
            nm_tok[s][qt] = OP(C, "dve", lambda e, s=s, qt=qt, r=r: e.tensor_copy(
                nm[s][:, qt:qt + 1, :], sml[:, r:r + 1, :]), deps=[tmx, nm_tok[s][qt]])
            texp = OP(C, "act", lambda e, s=s, sp_=sp_, r=r, qt=qt: e.activation(
                P_sb[r][:, :], sp_[:, :], AF.Exp, bias=nm[s][:, qt:qt + 1, 0],
                accum_out=lsum[s][:, qt:qt + 1, 0]),
                deps=[nm_tok[s][qt], l_tok[s][qt]])
            l_tok[s][qt] = texp
            tal = None
        else:
            tmin = OP(C, "dve", lambda e, s=s, qt=qt, r=r: e.tensor_tensor(
                sml[:, 2 + r:3 + r, :], nm[s][:, qt:qt + 1, :], sml[:, r:r + 1, :],
                ALU.min), deps=[tmx, nm_tok[s][qt]])
            tda = OP(C, "dve", lambda e, s=s, qt=qt, r=r: e.tensor_sub(
                sml[:, 4 + r:5 + r, :], sml[:, 2 + r:3 + r, :], nm[s][:, qt:qt + 1, :]),
                deps=[tmin])
            tal = OP(C, "act", lambda e, r=r: e.activation(
                sml[:, 6 + r:7 + r, :], sml[:, 4 + r:5 + r, :], AF.Exp), deps=[tda])
            nm_tok[s][qt] = OP(C, "dve", lambda e, s=s, qt=qt, r=r: e.tensor_copy(
                nm[s][:, qt:qt + 1, :], sml[:, 2 + r:3 + r, :]), deps=[tda])
            texp = OP(C, "act", lambda e, s=s, sp_=sp_, r=r, qt=qt: e.activation(
                P_sb[r][:, :], sp_[:, :], AF.Exp, bias=nm[s][:, qt:qt + 1, 0],
                accum_out=sml[:, 8 + r:9 + r, 0]),
                deps=[nm_tok[s][qt], l_tok[s][qt]])
            l_tok[s][qt] = OP(C, "dve", lambda e, s=s, qt=qt, r=r: e.scalar_tensor_tensor(
                lsum[s][:, qt:qt + 1, :], lsum[s][:, qt:qt + 1, :], sml[:, 6 + r:7 + r, 0],
                sml[:, 8 + r:9 + r, :], ALU.mult, ALU.add),
                deps=[texp, tal, l_tok[s][qt]])
        tcast = OP(C, "dve", lambda e, r=r: e.tensor_copy(Pb[r][:, :], P_sb[r][:, :]),
                   deps=[texp])
        return dict(kind="m", s=s, r=r, jg=jg, qt=qt, tcast=tcast, tal=tal)

    def emit_B(cx):
        nonlocal first_pv_guard
        if cx is None or cx["kind"] != "m":
            return
        s, r, jg, qt, tal = cx["s"], cx["r"], cx["jg"], cx["qt"], cx["tal"]
        ptp = pt_ps[r]; op_ = o_ps[r]
        trp = None
        for j2 in range(4):
            trp = TR(C, ptp[:, j2 * 128:(j2 + 1) * 128],
                     Pb[r][:, j2 * 128:(j2 + 1) * 128],
                     deps=[cx["tcast"]] if j2 == 0 else (),
                     inc="Epe" if j2 == 3 else None)
        tev = OP(C, "dve", lambda e, r=r, ptp=ptp: e.tensor_copy(
            ptb[r][:, :], ptp[:, :]), deps=[trp])
        tpv = None
        for half in range(2):
            for j2 in range(4):
                tpv = MM(C, op_[:, half * 512:(half + 1) * 512],
                         ptb[r][:, j2 * 128:(j2 + 1) * 128],
                         v_g[s][jg % 2][:, j2, half * 512:(half + 1) * 512],
                         start=(j2 == 0), stop=(j2 == 3),
                         deps=[tev, first_pv_guard] if (j2 == 0 and half == 0) else (),
                         inc="Epe" if (j2 == 3 and half == 1) else None)
        first_pv_guard = None
        if jg == 0:
            O_tok[s][qt] = OP(C, "act", lambda e, s=s, qt=qt, op_=op_: e.copy(
                O[s][qt][:, :], op_[:, :]), deps=[tpv, O_tok[s][qt]])
        else:
            O_tok[s][qt] = OP(C, "dve", lambda e, s=s, qt=qt, op_=op_, r=r: e.scalar_tensor_tensor(
                O[s][qt][:, :], O[s][qt][:, :], sml[:, 6 + r:7 + r, 0], op_[:, :],
                ALU.mult, ALU.add), deps=[tpv, tal, O_tok[s][qt]])

    def emit_E(s, qb, qt):
        r = s
        qtile = qb * 4 + qt
        op_ = o_ps[r]; ptp = pt_ps[r]
        p8 = 8 * r
        xl = pl.dma("sp" if s == 0 else "pool", xn_in[s][:, :],
                    C.v_xn_q[s, qtile], f"Dx{s}", deps=[xnin_free[s]])
        tinv = OP(C, "dve", lambda e, s=s, qt=qt, r=r: e.reciprocal(
            sml[:, 10 + r:11 + r, :], lsum[s][:, qt:qt + 1, :]), deps=[l_tok[s][qt]])
        l_tok[s][qt] = tinv
        tab = OP(C, "dve", lambda e, s=s, qt=qt, r=r: e.tensor_scalar_mul(
            at_bf[r][:, :], O[s][qt][:, :], sml[:, 10 + r:11 + r, 0]),
            deps=[tinv, O_tok[s][qt]])
        O_tok[s][qt] = tab
        evs = []
        for rnd in range(2):
            trp = None
            for j in range(4):
                trp = TR(C, ptp[:, j * 128:(j + 1) * 128],
                         at_bf[r][:, (rnd * 4 + j) * 128:(rnd * 4 + j + 1) * 128],
                         deps=([tab] if rnd == 0 else [evs[0]]) if j == 0 else (),
                         inc="Epe" if j == 3 else None)
            evs.append(OP(C, "dve", lambda e, rnd=rnd, r=r, ptp=ptp: e.tensor_copy(
                atT[r][:, rnd * 4:(rnd + 1) * 4, :],
                ptp[:, :].rearrange("p (c f) -> p c f", f=128)), deps=[trp]))
        tmo = None
        for half in range(2):
            for ic in range(8):
                tmo = MM(C, op_[:, half * 512:(half + 1) * 512],
                         atT[r][:, ic, :], w_o[:, ic, half * 512:(half + 1) * 512],
                         start=(ic == 0), stop=(ic == 7),
                         deps=[evs[1], tok_wo] if (ic == 0 and half == 0) else (),
                         inc="Epe" if (ic == 7 and half == 1) else None)
        t1 = OP(C, "dve", lambda e, s=s, op_=op_, r=r: e.scalar_tensor_tensor(
            x2a[r][:, :], xn_in[s][:, :], 1.0, op_[:, :], ALU.bypass, ALU.add),
            deps=[tmo, xl])
        xnin_free[s] = t1
        t2 = OP(C, "pool", lambda e, r=r: e.tensor_add(
            x2b[r][:, :], x2a[r][:, :], ob_b[:, :]),
            deps=[t1, tk_ob, x2_store[r]])
        x2_store[r] = pl.dma("sp", C.v_x2_q[s, qtile], x2b[r][:, :], f"Sx2{r}", deps=[t2])

        t_s1 = OP(C, "dve", lambda e, r=r, p8=p8: e.reduce_sum(
            st4[:, p8 + 0:p8 + 1, :], x2b[r][:, :], axis=AX.X), deps=[t2])
        t_sq = OP(C, "act", lambda e, r=r: e.activation(
            x2a[r][:, :], x2b[r][:, :], AF.Square), deps=[t2, t1])
        t_ssq = OP(C, "dve", lambda e, r=r, p8=p8: e.reduce_sum(
            st4[:, p8 + 1:p8 + 2, :], x2a[r][:, :], axis=AX.X), deps=[t_sq])
        t_m = OP(C, "dve", lambda e, p8=p8: e.tensor_scalar_mul(
            st4[:, p8 + 2:p8 + 3, :], st4[:, p8 + 0:p8 + 1, :], 1.0 / D), deps=[t_s1])
        t_v = OP(C, "dve", lambda e, p8=p8: e.tensor_scalar_mul(
            st4[:, p8 + 3:p8 + 4, :], st4[:, p8 + 1:p8 + 2, :], 1.0 / D), deps=[t_ssq])
        t_mm = OP(C, "dve", lambda e, p8=p8: e.tensor_mul(
            st4[:, p8 + 4:p8 + 5, :], st4[:, p8 + 2:p8 + 3, :], st4[:, p8 + 2:p8 + 3, :]),
            deps=[t_m])
        t_v = OP(C, "dve", lambda e, p8=p8: e.tensor_sub(
            st4[:, p8 + 3:p8 + 4, :], st4[:, p8 + 3:p8 + 4, :], st4[:, p8 + 4:p8 + 5, :]),
            deps=[t_v, t_mm])
        t_l = OP(C, "act", lambda e, p8=p8: e.activation(
            st4[:, p8 + 3:p8 + 4, :], st4[:, p8 + 3:p8 + 4, :], AF.Ln, bias=1e-5),
            deps=[t_v, C.tok_consts])
        t_r = OP(C, "act", lambda e, p8=p8: e.activation(
            st4[:, p8 + 5:p8 + 6, :], st4[:, p8 + 3:p8 + 4, :], AF.Exp, scale=-0.5),
            deps=[t_l])
        thn = OP(C, "dve", lambda e, r=r, p8=p8: e.tensor_scalar(
            hnb[r][:, :], x2b[r][:, :], st4[:, p8 + 2:p8 + 3, :], st4[:, p8 + 5:p8 + 6, :],
            ALU.subtract, ALU.mult), deps=[t_r])
        hev = []
        for rnd in range(2):
            trp = None
            for j in range(4):
                trp = TR(C, ptp[:, j * 128:(j + 1) * 128],
                         hnb[r][:, (rnd * 4 + j) * 128:(rnd * 4 + j + 1) * 128],
                         deps=([thn, tmo, evs[1]] if rnd == 0 else [hev[3]]) if j == 0 else (),
                         inc="Epe" if j == 3 else None)
            for ic4 in range(4):
                ic = rnd * 4 + ic4
                hev.append(OP(C, "dve", lambda e, r=r, ic=ic, ic4=ic4, ptp=ptp: e.tensor_scalar(
                    hT_st[r][:, ic, :], ptp[:, ic4 * 128:(ic4 + 1) * 128],
                    t_g2c[:, ic:ic + 1], t_b2c[:, ic:ic + 1], ALU.mult, ALU.add),
                    deps=[trp, hT_store[r], trg, trb]))
        hT_store[r] = pl.dma("sp", C.v_hT[s, :, :, qtile * 128:(qtile + 1) * 128],
                             hT_st[r][:, :, :], f"ShT{r}", deps=hev)
        return dict(kind="e")

    def stream_steps(s):
        out = []
        for qb in range(8):
            for jg in range(qb + 1):
                for qt in range(4):
                    out.append(("m", s, qb, jg, qt))
            for qt in range(4):
                out.append(("e", s, qb, None, qt))
        return out

    st0, st1 = stream_steps(0), stream_steps(1)
    glob = []
    for i in range(len(st0)):
        glob.append(st0[i])
        glob.append(st1[i])

    prev_cx = None
    for (kind, s, qb, jg, qt) in glob:
        td.start()
        cx = emit_A(s, qb, jg, qt) if kind == "m" else emit_E(s, qb, qt)
        emit_B(prev_cx)
        prev_cx = cx
        td.end()
    td.start()
    emit_B(prev_cx)
    td.end()


def plan_p4(C):
    nc, pl = C.nc, C.pl

    def sb(shape, dt):
        C._uid = getattr(C, "_uid", 0) + 1
        return nc.alloc_sbuf_tensor(f"sb{C._uid}", list(shape), dt)

    def ps(shape, dt):
        C._uid = getattr(C, "_uid", 0) + 1
        return nc.alloc_psum_tensor(f"ps{C._uid}", list(shape), dt)

    w1 = sb((128, 8, 32, 128), BF16)
    w2 = sb((128, 32, D), BF16)
    b1 = sb((128, 32), F32)
    b2_b = sb((128, D), F32)
    u_ps = [ps((128, 512), F32), ps((128, 512), F32)]
    y_ps = [ps((128, D), F32), ps((128, D), F32)]

    tok_w1 = pl.dma("pool", w1[:, :, :, :],
                    C.i_w1m.ap().rearrange("(c p) (m f) -> p c m f", p=128, f=128), "Dw")
    tok_w2 = pl.dma("pool", w2[:, :, :],
                    C.i_w2m.ap().rearrange("(m p) o -> p m o", p=128), "Dw")
    tok_b1 = pl.dma("sp", b1[:, :], C.i_b1m_pc.ap(), "Dc")
    scb2 = scope(nc); scb2.__enter__()
    r_b2 = sb((1, D), F32)
    trb2 = pl.dma("sp", r_b2[0:1, :], C.i_b2mrow.ap(), "Dc")
    tk_b2 = bcast_row(C, r_b2, b2_b, y_ps[0], deps=[trb2, C.tok_consts])
    scb2.__exit__(None, None, None)

    hT_in = [sb((128, 8, 512), BF16), sb((128, 8, 512), BF16)]
    g_st = sb((128, 32, 512), BF16)
    x2_in = [sb((128, D), F32), sb((128, D), F32)]
    x3_st = [sb((128, D), F32)]

    td = TD(C, "T4")
    g_free = tok_w1
    u_free = [None, None]
    y_free = [tk_b2, tk_b2]
    x2_used = [None, None]
    x3_store = [None]

    for i in range(16):
        s, tt = i // 8, i % 8
        td.start()
        th = pl.dma("sp", hT_in[i % 2][:, :, :],
                    C.v_hT[s, :, :, tt * 512:(tt + 1) * 512], f"Dq{i % 2}",
                    deps=[tk_b2] if i == 0 else ())
        gel = []
        for fc in range(32):
            tu = None
            for ic in range(8):
                tu = MM(C, u_ps[fc % 2][:, :], w1[:, ic, fc, :],
                        hT_in[i % 2][:, ic, :], start=(ic == 0), stop=(ic == 7),
                        deps=[th, tok_w1, u_free[fc % 2], g_free] if ic == 0 else (),
                        inc="Epe" if ic == 7 else None)
            tg = OP(C, "act", lambda e, fc=fc: e.activation(
                g_st[:, fc, :], u_ps[fc % 2][:, :], AF.Gelu, bias=b1[:, fc:fc + 1]),
                deps=[tu, tok_b1])
            u_free[fc % 2] = tg
            gel.append(tg)
        last_mm = None
        for ts4 in range(4):
            q = tt * 4 + ts4
            x2l = pl.dma("pool", x2_in[ts4 % 2][:, :], C.v_x2_q[s, q], f"Dx{ts4 % 2}",
                         deps=[x2_used[ts4 % 2]])
            ty = None
            for half in range(2):
                for fc in range(32):
                    ty = MM(C, y_ps[ts4 % 2][:, half * 512:(half + 1) * 512],
                            g_st[:, fc, ts4 * 128:(ts4 + 1) * 128],
                            w2[:, fc, half * 512:(half + 1) * 512],
                            start=(fc == 0), stop=(fc == 31),
                            deps=[gel[-1], tok_w2, y_free[ts4 % 2]] if (fc == 0 and half == 0) else (),
                            inc="Epe" if (fc == 31 and half == 1) else None)
            last_mm = ty
            t1 = OP(C, "dve", lambda e, ts4=ts4: e.scalar_tensor_tensor(
                x3_st[0][:, :], x2_in[ts4 % 2][:, :], 1.0, y_ps[ts4 % 2][:, :],
                ALU.bypass, ALU.add), deps=[ty, x2l, x3_store[0]])
            x2_used[ts4 % 2] = t1
            y_free[ts4 % 2] = t1
            t2 = OP(C, "dve", lambda e: e.tensor_add(
                x3_st[0][:, :], x3_st[0][:, :], b2_b[:, :]), deps=[t1, tk_b2])
            x3_store[0] = pl.dma("sp", C.v_x3_q[s, q], x3_st[0][:, :],
                                 "Sx3", deps=[t2])
        g_free = last_mm
        td.end()


def plan_p5(C):
    nc, pl = C.nc, C.pl

    def sb(shape, dt):
        C._uid = getattr(C, "_uid", 0) + 1
        return nc.alloc_sbuf_tensor(f"sb{C._uid}", list(shape), dt)

    def ps(shape, dt):
        C._uid = getattr(C, "_uid", 0) + 1
        return nc.alloc_psum_tensor(f"ps{C._uid}", list(shape), dt)

    w1t = sb((128, 8, 32, 128), BF16)
    w2t = sb((128, 8, 32, 128), BF16)
    s1c = sb((128, 32), F32)
    s2c = sb((128, 32), F32)
    u1 = [ps((128, 512), F32), ps((128, 512), F32)]
    u2 = [ps((128, 512), F32), ps((128, 512), F32)]

    tok_w1 = pl.dma("pool", w1t[:, :, :, :],
                    C.i_fw1t.ap().rearrange("(c p) (m f) -> p c m f", p=128, f=128), "Dw")
    tok_w2 = pl.dma("pool", w2t[:, :, :, :],
                    C.i_fw2t.ap().rearrange("(c p) (m f) -> p c m f", p=128, f=128), "Dw")
    tok_s1 = pl.dma("sp", s1c[:, :], C.i_s1_pc.ap(), "Dc")
    tok_s2 = pl.dma("sp", s2c[:, :], C.i_s2_pc.ap(), "Dc")

    xnT_in = [sb((128, 8, 512), BF16), sb((128, 8, 512), BF16)]
    h_st = sb((128, 32, 512), BF16)
    t1f = [sb((128, 512), F32), sb((128, 512), F32)]
    t2f = [sb((128, 512), F32), sb((128, 512), F32)]

    td = TD(C, "T5")
    u1_free = [None, None]
    u2_free = [None, None]
    h_free = None
    t1_free = [None, None]
    t2_free = [None, None]

    for i in range(16):
        s, tt = i // 8, i % 8
        td.start()
        tx = pl.dma("sp", xnT_in[i % 2][:, :, :],
                    C.v_xnT[s, :, :, tt * 512:(tt + 1) * 512], f"Dq{i % 2}")
        evs = []
        for oc in range(32):
            tu1 = None
            for ic in range(8):
                tu1 = MM(C, u1[oc % 2][:, :], w1t[:, ic, oc, :], xnT_in[i % 2][:, ic, :],
                         start=(ic == 0), stop=(ic == 7),
                         deps=[tx, tok_w1, u1_free[oc % 2]] if ic == 0 else (),
                         inc="Epe" if ic == 7 else None)
            tu2 = None
            for ic in range(8):
                tu2 = MM(C, u2[oc % 2][:, :], w2t[:, ic, oc, :], xnT_in[i % 2][:, ic, :],
                         start=(ic == 0), stop=(ic == 7),
                         deps=[tok_w2, u2_free[oc % 2]] if ic == 0 else (),
                         inc="Epe" if ic == 7 else None)
            ts1 = OP(C, "act", lambda e, oc=oc: e.activation(
                t1f[oc % 2][:, :], u1[oc % 2][:, :], AF.Silu, scale=s1c[:, oc:oc + 1]),
                deps=[tu1, tok_s1, t1_free[oc % 2]])
            u1_free[oc % 2] = ts1
            ts2 = OP(C, "dve", lambda e, oc=oc: e.tensor_scalar_mul(
                t2f[oc % 2][:, :], u2[oc % 2][:, :], s2c[:, oc:oc + 1]),
                deps=[tu2, tok_s2, t2_free[oc % 2]])
            u2_free[oc % 2] = ts2
            tm = OP(C, "dve", lambda e, oc=oc: e.tensor_mul(
                h_st[:, oc, :], t1f[oc % 2][:, :], t2f[oc % 2][:, :]),
                deps=[ts1, ts2, h_free])
            t1_free[oc % 2] = tm
            t2_free[oc % 2] = tm
            evs.append(tm)
        h_free = pl.dma("sp", C.v_h[s, tt // 2, :, :, (tt % 2) * 512:(tt % 2) * 512 + 512],
                        h_st[:, :, :], "Sh", deps=evs)
        td.end()


def plan_p67(C, layer):
    nc, pl = C.nc, C.pl

    def sb(shape, dt):
        C._uid = getattr(C, "_uid", 0) + 1
        return nc.alloc_sbuf_tensor(f"sb{C._uid}", list(shape), dt)

    def ps(shape, dt):
        C._uid = getattr(C, "_uid", 0) + 1
        return nc.alloc_psum_tensor(f"ps{C._uid}", list(shape), dt)

    srcv = C.v_h if layer == 1 else C.v_g1
    dstv = C.v_g1 if layer == 1 else C.v_g2
    wT = C.d_h1R if layer == 1 else C.d_h2R
    scin = C.i_h1s_pc if layer == 1 else C.i_h2s_pc

    sb((128, H4), BF16)
    hin = [sb((128, 32, 1024), BF16), sb((128, 32, 1024), BF16)]
    wsl = [sb((128, H4), BF16) for _ in range(3)]
    go = [sb((128, 512), BF16), sb((128, 512), BF16)]
    sc_ = sb((128, 32), F32)
    g_ps = [ps((128, 512), F32) for _ in range(4)]

    tok_sc = pl.dma("sp", sc_[:, :], scin.ap(), "Dc")
    wv = wT.ap()

    td = TD(C, "T6" if layer == 1 else "T7")
    gfree = [None] * 4
    go_store = [None, None]
    thi = [None, None]
    NU = 256
    WQ = [("pool", 0, 2048), ("act", 2048, 3072), ("sp", 3072, 4096)]

    def load_wsl(u, dep):
        toks = []
        for qn, a, b in WQ:
            toks.append(pl.dma(
                qn, wsl[u % 3][:, a:b], wv[u % 32][:, a:b], f"Dwl{qn}",
                deps=[dep] if dep is not None else ()))
        return toks

    thi[0] = pl.dma("pool", hin[0][:, :, :], srcv[0, 0], "Dh0")
    wtoks = {0: load_wsl(0, None), 1: load_wsl(1, None)}
    mm_last = {}
    for g in range(8):
        s, off = g // 4, (g % 4) * 1024
        for oc in range(32):
            u = g * 32 + oc
            td.start()
            if u + 2 < NU:
                wtoks[u + 2] = load_wsl(u + 2, mm_last.get(u - 1))
            if oc == 1 and g + 1 < 8:
                sn = (g + 1) // 4
                thi[(g + 1) % 2] = pl.dma(
                    "pool", hin[(g + 1) % 2][:, :, :],
                    srcv[sn, (g + 1) % 4], f"Dh{(g + 1) % 2}")
            for half in range(2):
                idx = (oc * 2 + half) % 4
                tm = None
                for ic in range(32):
                    tm = MM(C, g_ps[idx][:, :], wsl[u % 3][:, ic * 128:(ic + 1) * 128],
                            hin[g % 2][:, ic, half * 512:(half + 1) * 512],
                            start=(ic == 0), stop=(ic == 31),
                            deps=(wtoks[u] + [thi[g % 2], gfree[idx]]) if ic == 0 else (),
                            inc="Epe" if ic == 31 else None)
                tg = OP(C, "act", lambda e, half=half, idx=idx, oc=oc: e.activation(
                    go[half][:, :], g_ps[idx][:, :], AF.Gelu, scale=sc_[:, oc:oc + 1]),
                    deps=[tm, tok_sc, go_store[half]])
                gfree[idx] = tg
                go_store[half] = pl.dma(
                    "sp", dstv[s, g % 4, :, oc, half * 512:(half + 1) * 512],
                    go[half][:, :], f"Sg{half}", deps=[tg])
            mm_last[u] = tm
            td.end()


def plan_p8(C):
    nc, pl = C.nc, C.pl

    def sb(shape, dt):
        C._uid = getattr(C, "_uid", 0) + 1
        return nc.alloc_sbuf_tensor(f"sb{C._uid}", list(shape), dt)

    def ps(shape, dt):
        C._uid = getattr(C, "_uid", 0) + 1
        return nc.alloc_psum_tensor(f"ps{C._uid}", list(shape), dt)

    w3 = sb((128, 32, D), BF16)
    s3r = sb((1, D), F32)
    s3b = sb((128, D), F32)
    y_ps = [ps((128, D), F32), ps((128, D), F32)]

    tok_w3 = pl.dma("pool", w3[:, :, :],
                    C.i_fw3t.ap().rearrange("(m p) o -> p m o", p=128), "Dw")
    trs3 = pl.dma("sp", s3r[0:1, :], C.i_s3row.ap(), "Dc")
    tk_s3 = bcast_row(C, s3r, s3b, y_ps[0], deps=[trs3, C.tok_consts])

    g2in = [sb((128, 32, 512), BF16), sb((128, 32, 512), BF16)]
    x3in = [sb((128, D), F32), sb((128, D), F32)]
    xin = [sb((128, D), F32), sb((128, D), F32)]
    tmpa = [sb((128, D), F32), sb((128, D), F32)]
    outst = sb((128, 4, D), F32)

    td = TD(C, "T8")
    y_free = [tk_s3, tk_s3]
    x3_used = [None, None]
    x_used = [None, None]
    tmp_used = [None, None]
    out_store = None
    tg = None

    for i in range(16):
        s, tt = i // 8, i % 8
        td.start()
        tg = pl.dma("sp", g2in[i % 2][:, :, :],
                    C.v_g2[s, tt // 2, :, :, (tt % 2) * 512:(tt % 2) * 512 + 512],
                    f"Dq{i % 2}")
        evs = []
        for ts4 in range(4):
            q = tt * 4 + ts4
            tx3 = pl.dma("pool", x3in[ts4 % 2][:, :], C.v_x3_q[s, q], f"Dx{ts4 % 2}",
                         deps=[x3_used[ts4 % 2]])
            tx = pl.dma("pool", xin[ts4 % 2][:, :], C.v_x_q[s, q], f"Dx2{ts4 % 2}",
                        deps=[x_used[ts4 % 2]])
            ty = None
            for half in range(2):
                for fc in range(32):
                    ty = MM(C, y_ps[ts4 % 2][:, half * 512:(half + 1) * 512],
                            g2in[i % 2][:, fc, ts4 * 128:(ts4 + 1) * 128],
                            w3[:, fc, half * 512:(half + 1) * 512],
                            start=(fc == 0), stop=(fc == 31),
                            deps=[tg, tok_w3, y_free[ts4 % 2]] if (fc == 0 and half == 0) else (),
                            inc="Epe" if (fc == 31 and half == 1) else None)
            t1 = OP(C, "dve", lambda e, ts4=ts4: e.scalar_tensor_tensor(
                tmpa[ts4 % 2][:, :], s3b[:, :], 1.0, y_ps[ts4 % 2][:, :],
                ALU.bypass, ALU.mult), deps=[ty, tk_s3, tmp_used[ts4 % 2]])
            y_free[ts4 % 2] = t1
            t2 = OP(C, "dve", lambda e, ts4=ts4: e.tensor_add(
                tmpa[ts4 % 2][:, :], tmpa[ts4 % 2][:, :], x3in[ts4 % 2][:, :]),
                deps=[t1, tx3])
            x3_used[ts4 % 2] = t2
            t3 = OP(C, "dve", lambda e, ts4=ts4: e.tensor_add(
                outst[:, ts4, :], tmpa[ts4 % 2][:, :], xin[ts4 % 2][:, :]),
                deps=[t2, tx, out_store])
            x_used[ts4 % 2] = t3
            tmp_used[ts4 % 2] = t3
            evs.append(t3)
        out_store = pl.dma("act", C.v_out_t[s, tt], outst[:, :, :], "Sout", deps=evs)
        td.end()


def _host_consts():
    i = np.arange(128)
    tri = (i[:, None] <= i[None, :]).astype(np.float32)
    consts = {
        "ctri_n": -tri,
        "ctri_p": tri,
        "cnegrow": -np.ones((1, 128), np.float32),
        "cposrow": np.ones((1, 128), np.float32),
        "conescol": np.ones((128, 1), np.float32),
        "cident": np.eye(128, dtype=BF),
        "cmasktri": np.where(i[None, :] <= i[:, None], 0.0, NEG).astype(np.float32),
        "cbias": np.broadcast_to(np.array([1e-5, 1.0, 1e-8], np.float32), (128, 3)).copy(),
    }
    freqs = np.exp(-np.arange(HALF, dtype=np.float32) * (math.log(10000.0) / HALF))
    th = np.arange(T, dtype=np.float32)[:, None] * freqs[None, :]
    consts["ccos"] = np.cos(th).astype(np.float32)
    consts["csin"] = np.sin(th).astype(np.float32)
    return consts


def _layout_inputs(inputs):
    f = np.ascontiguousarray
    perm = np.concatenate([np.arange(0, D, 2), np.arange(1, D, 2)])
    d = {
        "wq": inputs["L0_attn_Wq"][:, perm], "wk": inputs["L0_attn_Wk"][:, perm],
        "wv": inputs["L0_attn_Wv"],
        "wgate": inputs["L0_attn_gate_W"][:, perm], "wp": inputs["L0_attn_p_W"][:, perm],
        "wclock": inputs["L0_attn_clock_W"], "wo": inputs["L0_attn_out_W"],
        "w1m": inputs["L0_attn_mlp_W1"], "w2m": inputs["L0_attn_mlp_W2"],
        "fw1t": f(inputs["L0_ffn_w1_w"].T), "fw2t": f(inputs["L0_ffn_w2_w"].T),
        "fh1r": f(inputs["L0_ffn_h1_w"].T.reshape(32, 128, 32, 128).transpose(2, 1, 0, 3)),
        "fh2r": f(inputs["L0_ffn_h2_w"].T.reshape(32, 128, 32, 128).transpose(2, 1, 0, 3)),
        "fw3t": f(inputs["L0_ffn_w3_w"].T),
        "gbrow": inputs["L0_attn_gate_b"][perm].reshape(1, D),
        "pbrow": inputs["L0_attn_p_b"][perm].reshape(1, D),
        "obrow": inputs["L0_attn_out_b"].reshape(1, D),
        "b2mrow": inputs["L0_attn_mlp_b2"].reshape(1, D),
        "s3row": inputs["L0_ffn_w3_s"].reshape(1, D),
        "cbrow": inputs["L0_attn_clock_b"].reshape(1, 1),
        "g1c": f(inputs["L0_attn_ln1_g"].reshape(8, 128).T),
        "b1c": f(inputs["L0_attn_ln1_b"].reshape(8, 128).T),
        "g2c": f(inputs["L0_attn_ln2_g"].reshape(8, 128).T),
        "b2c": f(inputs["L0_attn_ln2_b"].reshape(8, 128).T),
        "b1m_pc": f(inputs["L0_attn_mlp_b1"].reshape(32, 128).T),
        "s1_pc": f(inputs["L0_ffn_w1_s"].reshape(32, 128).T),
        "s2_pc": f(inputs["L0_ffn_w2_s"].reshape(32, 128).T),
        "h1s_pc": f(inputs["L0_ffn_h1_s"].reshape(32, 128).T),
        "h2s_pc": f(inputs["L0_ffn_h2_s"].reshape(32, 128).T),
    }
    return {k: np.ascontiguousarray(np.asarray(v, dtype=np.float32)) for k, v in d.items()}


_PASSES = {"p0", "p1", "p3", "p4", "p5", "p6", "p8"}


def kernel(**inputs):
    cfg = SimpleNamespace(passes=_PASSES)
    nc, C = build_nc(cfg)
    x = np.asarray(inputs["L0_attn_x"], dtype=np.float32)
    shared = _layout_inputs(inputs)
    shared.update(_host_consts())
    in_maps = []
    for c in range(NCORES):
        m = dict(shared)
        m["x"] = np.ascontiguousarray(x[c * 2:(c + 1) * 2])
        in_maps.append(m)
    res = run_bass_kernel_spmd(nc, in_maps, core_ids=list(range(NCORES)))
    return np.concatenate([r["out"] for r in res.results], axis=0)


if __name__ == "__main__":
    import sys
    passes = set(sys.argv[1].split(",")) if len(sys.argv) > 1 else _PASSES
    cfg = SimpleNamespace(passes=passes)
    nc, C = build_nc(cfg)
    print("n_instructions:", nc.n_instructions())
